# version 02
# baseline (speedup 1.0000x reference)
import os
import numpy as np
import ml_dtypes
import concourse.bass as bass
import concourse.mybir as mybir
from concourse.bass_utils import run_bass_kernel_spmd

F32 = mybir.dt.float32
BF16 = mybir.dt.bfloat16
I8 = mybir.dt.int8
AF = mybir.ActivationFunctionType
ALU = mybir.AluOpType
AX = mybir.AxisListType
BFNP = ml_dtypes.bfloat16

S = 2048
H = 4096
D = 128
NH = 32
NKV = 8
INTER = 11008
NIT = INTER // 128
SB = 256
NCH = S // SB
EXP_SHIFT = 25.0
SCALE = 1.0 / float(np.sqrt(np.float32(D)))

ENGINES = ("tensor", "vector", "scalar", "sync", "gpsimd")

SEM_NAMES = [
    "mm", "ve", "ac",
    "ld_s", "sp", "so",
    "g_xbt",
    "g_wsl0", "g_wsl1",
    "g_vsl",
    "g_wo0", "g_wo1", "g_wo2", "g_wo3", "g_wo4",
    "g_wg0", "g_wg1",
    "g_wd0", "g_wd1",
    "bar",
]


class Ctx:
    def __init__(self, nc, eng_name, sems):
        self.nc = nc
        self.eng = eng_name
        self.e = {"tensor": nc.tensor, "vector": nc.vector, "scalar": nc.scalar,
                  "sync": nc.sync, "gpsimd": nc.gpsimd}[eng_name]
        self.sems = sems
        self.cnt = {n: 0 for n in sems}
        self.nbar = 0

    def on(self, name):
        return self.eng == name

    def inc(self, ins, sem, n=1):
        self.cnt[sem] += n
        if ins is not None:
            ins.then_inc(self.sems[sem], n)
        return self.cnt[sem]

    def wait(self, owner, sem, val):
        if self.eng == owner and val > 0:
            self.e.wait_ge(self.sems[sem], val)

    def dma(self, owner, sem, dst, src, transpose=False):
        ins = None
        if self.on(owner):
            eng = {"sync": self.nc.sync, "gpsimd": self.nc.gpsimd}[owner]
            if transpose:
                ins = eng.dma_start(out=dst, in_=src, transpose=True)
            else:
                ins = eng.dma_start(out=dst, in_=src)
        return self.inc(ins, sem, 16)

    def barrier(self):
        self.nbar += 1
        self.cnt["bar"] += 5
        self.e.sem_inc(self.sems["bar"], 1)
        self.e.wait_ge(self.sems["bar"], 5 * self.nbar)


def build_program(debug_scratch=False):
    nc = bass.Bass("TRN2", target_bir_lowering=False, debug=False)
    SK = "ExternalOutput" if debug_scratch else "Internal"

    x_nat = nc.dram_tensor("x_nat", [S, H], F32, kind="ExternalInput")
    xT = nc.dram_tensor("xT", [H, S], F32, kind="ExternalInput")
    wq_t = nc.dram_tensor("wq_t", [32, 128, 32, 128], F32, kind="ExternalInput")
    wk_t = nc.dram_tensor("wk_t", [8, 128, 32, 128], F32, kind="ExternalInput")
    wv_t = nc.dram_tensor("wv_t", [4, 128, 32, 256], F32, kind="ExternalInput")
    wo_t = nc.dram_tensor("wo_t", [8, 128, 32, 512], F32, kind="ExternalInput")
    wg_t = nc.dram_tensor("wg_t", [NIT, 128, 32, 128], I8, kind="ExternalInput")
    wu_t = nc.dram_tensor("wu_t", [NIT, 128, 32, 128], I8, kind="ExternalInput")
    wd_t = nc.dram_tensor("wd_t", [8, NIT, 128, 512], I8, kind="ExternalInput")
    sg_t = nc.dram_tensor("sg_t", [128, NIT], F32, kind="ExternalInput")
    su_t = nc.dram_tensor("su_t", [128, NIT], F32, kind="ExternalInput")
    sdb_in = nc.dram_tensor("sdb_in", [128, H], F32, kind="ExternalInput")
    cos_t = nc.dram_tensor("cos_t", [128, S], BF16, kind="ExternalInput")
    sin_t = nc.dram_tensor("sin_t", [128, S], BF16, kind="ExternalInput")
    tri_t = nc.dram_tensor("tri_t", [128, 128], BF16, kind="ExternalInput")
    onc_t = nc.dram_tensor("onc_t", [128, 1], BF16, kind="ExternalInput")
    onr_t = nc.dram_tensor("onr_t", [1, 128], BF16, kind="ExternalInput")

    out = nc.dram_tensor("out", [S, H], F32, kind="ExternalOutput")

    qt_s = nc.dram_tensor("qt_s", [NH, 128, S], BF16, kind=SK)
    v_s = nc.dram_tensor("v_s", [NKV, 16, 128, 128], BF16, kind=SK)
    y_s = nc.dram_tensor("y_s", [S, H], BF16, kind=SK)
    x1_s = nc.dram_tensor("x1_s", [S, H], BF16, kind=SK)
    if debug_scratch:
        dbg_x1t = nc.dram_tensor("dbg_x1t", [128, 32 * SB], BF16, kind="ExternalOutput")
        dbg_h = nc.dram_tensor("dbg_h", [128, NIT * SB], BF16, kind="ExternalOutput")
        dbg_y2 = nc.dram_tensor("dbg_y2", [128, 2 * H], BF16, kind="ExternalOutput")
        dbg_wd = nc.dram_tensor("dbg_wd", [2, 128, 11 * 512], BF16, kind="ExternalOutput")
        dbg_h2 = nc.dram_tensor("dbg_h2", [128, NIT * SB], BF16, kind="ExternalOutput")
        dbg_sdb = nc.dram_tensor("dbg_sdb", [128, H], F32, kind="ExternalOutput")
        dbg_sdb0 = nc.dram_tensor("dbg_sdb0", [128, H], F32, kind="ExternalOutput")
        dbg_sdb1 = nc.dram_tensor("dbg_sdb1", [128, H], F32, kind="ExternalOutput")
        dbg_dn = nc.dram_tensor("dbg_dn", [2, 128, 512], F32, kind="ExternalOutput")
    else:
        dbg_x1t = dbg_h = dbg_y2 = dbg_wd = dbg_h2 = dbg_sdb = dbg_dn = None
        dbg_sdb0 = dbg_sdb1 = None

    KT = nc.alloc_sbuf_tensor("KT", [128, NKV * S], BF16)
    sgb = nc.alloc_sbuf_tensor("sgb", [128, NIT], F32)
    sub = nc.alloc_sbuf_tensor("sub", [128, NIT], F32)
    sdb = nc.alloc_sbuf_tensor("sdb", [128, H], F32)
    trib = nc.alloc_sbuf_tensor("trib", [128, 128], BF16)
    oncb = nc.alloc_sbuf_tensor("oncb", [128, 1], BF16)
    onrb = nc.alloc_sbuf_tensor("onrb", [1, 128], BF16)
    negB = nc.alloc_sbuf_tensor("negB", [128, 1], F32)
    epsb = nc.alloc_sbuf_tensor("epsb", [128, 1], F32)

    mark = nc.sbuf_base

    xbT = nc.alloc_sbuf_tensor("xbT", [128, 32 * S], BF16)
    a_sub = nc.sbuf_base
    cosb = nc.alloc_sbuf_tensor("cosb", [128, S], BF16)
    sinb = nc.alloc_sbuf_tensor("sinb", [128, S], BF16)
    wslab = [nc.alloc_sbuf_tensor(f"wslab{i}", [128, 2048], BF16) for i in range(2)]
    qstage = [nc.alloc_sbuf_tensor(f"qstage{i}", [128, S], BF16) for i in range(2)]
    rt_t1 = nc.alloc_sbuf_tensor("rt_t1", [128, 512], BF16)
    rt_t2 = nc.alloc_sbuf_tensor("rt_t2", [128, 512], BF16)
    a_peak = nc.sbuf_base
    nc.sbuf_base = a_sub
    vslab = nc.alloc_sbuf_tensor("vslab", [128, 32 * 256], BF16)
    vstage = [nc.alloc_sbuf_tensor(f"vstage{i}", [128, 256], BF16) for i in range(2)]
    a_peak = max(a_peak, nc.sbuf_base)
    nc.sbuf_base = mark

    OTsl = nc.alloc_sbuf_tensor("OTsl", [128, NH * 512], BF16)
    qh = [nc.alloc_sbuf_tensor(f"qh{i}", [128, 512], BF16) for i in range(2)]
    vg = [nc.alloc_sbuf_tensor(f"vg{i}", [128, 16 * 128], BF16) for i in range(2)]
    pt = [nc.alloc_sbuf_tensor(f"pt{i}", [128, 512], BF16) for i in range(4)]
    woq = [nc.alloc_sbuf_tensor(f"woq{i}", [128, 8 * 512], BF16) for i in range(5)]
    cstage = [nc.alloc_sbuf_tensor(f"cstage{i}", [128, 512], BF16) for i in range(4)]
    recip = nc.alloc_sbuf_tensor("recip", [1, 512], BF16)
    recipb = nc.alloc_sbuf_tensor("recipb", [128, 512], F32)
    ybuf = nc.alloc_sbuf_tensor("ybuf", [128, H], BF16)
    xzbuf = nc.alloc_sbuf_tensor("xzbuf", [128, H], F32)
    x1buf = nc.alloc_sbuf_tensor("x1buf", [128, H], BF16)
    sm = {n: nc.alloc_sbuf_tensor(f"sm_{n}", [128, 1], F32)
          for n in ("s1", "s2", "mean", "m2", "var", "lnv", "rstd", "nmr")}
    bc_peak = nc.sbuf_base
    nc.sbuf_base = mark

    hstore = nc.alloc_sbuf_tensor("hstore", [128, NIT * SB], BF16)
    wdring = [nc.alloc_sbuf_tensor(f"wdring{i}", [128, 11 * 512], BF16) for i in range(2)]
    wgr = [nc.alloc_sbuf_tensor(f"wgr{i}", [128, 4096], BF16) for i in range(2)]
    wur = [nc.alloc_sbuf_tensor(f"wur{i}", [128, 4096], BF16) for i in range(2)]
    x1T = nc.alloc_sbuf_tensor("x1T", [128, 32 * SB], BF16, is_transpose_dest=True)
    t1buf = nc.alloc_sbuf_tensor("t1buf", [128, SB], BF16)
    yst = nc.alloc_sbuf_tensor("yst", [128, 2 * H], BF16)
    x1re = nc.alloc_sbuf_tensor("x1re", [128, H], BF16)
    zbuf = nc.alloc_sbuf_tensor("zbuf", [128, H], F32)
    outst = nc.alloc_sbuf_tensor("outst", [128, H // 4], F32)
    sm2 = {n: nc.alloc_sbuf_tensor(f"sm2_{n}", [128, 1], F32)
           for n in ("s1", "s2", "mean", "m2", "var", "lnv", "rstd", "nmr")}
    d_peak = nc.sbuf_base
    nc.sbuf_base = mark

    assert max(a_peak, bc_peak, d_peak) <= nc.sbuf_top, \
        (a_peak, bc_peak, d_peak, nc.sbuf_top)

    pmark = nc.psum_base
    qk_ps = [nc.alloc_psum_tensor(f"qk_ps{i}", [128, 512], F32) for i in range(3)]
    nc.psum_base = pmark
    st_ps = [nc.alloc_psum_tensor(f"st_ps{i}", [128, 512], F32) for i in range(2)]
    ot_ps = [nc.alloc_psum_tensor(f"ot_ps{i}", [128, 512], F32) for i in range(2)]
    den_ps = [nc.alloc_psum_tensor(f"den_ps{i}", [1, 512], F32) for i in range(2)]
    bc_ps = nc.alloc_psum_tensor("bc_ps", [128, 512], F32)
    nc.psum_base = pmark
    gate_ps = nc.alloc_psum_tensor("gate_ps", [128, SB], F32)
    up_ps = nc.alloc_psum_tensor("up_ps", [128, SB], F32)
    dn_ps = [nc.alloc_psum_tensor(f"dn_ps{i}", [128, 512], F32) for i in range(2)]
    nc.psum_base = pmark

    sems = {n: nc.alloc_semaphore(name=f"s_{n}") for n in SEM_NAMES}

    def emit(c: Ctx):
        nc_ = c.nc

        v_tri = c.dma("sync", "ld_s", trib[:], tri_t.ap())
        v_onc = c.dma("sync", "ld_s", oncb[:], onc_t.ap())
        v_onr = c.dma("sync", "ld_s", onrb[:], onr_t.ap())
        v_sg = c.dma("sync", "ld_s", sgb[:], sg_t.ap())
        v_su = c.dma("sync", "ld_s", sub[:], su_t.ap())
        v_sdr = c.dma("sync", "ld_s", sdb[:], sdb_in.ap())
        v_cos = c.dma("sync", "ld_s", cosb[:], cos_t.ap())
        v_sin = c.dma("sync", "ld_s", sinb[:], sin_t.ap())
        ins = None
        if c.on("vector"):
            nc_.vector.memset(negB[:], -EXP_SHIFT)
            ins = nc_.vector.memset(epsb[:], 1e-5)
        v_msets = c.inc(ins, "ve")

        if dbg_sdb0 is not None:
            c.wait("sync", "sp", 0)
            c.dma("sync", "sp", dbg_sdb0.ap(), sdb[:])

        c.barrier()

        v_xbt = 0
        for cc in range(32):
            v_xbt = c.dma("gpsimd", "g_xbt", xbT[:, cc * S:(cc + 1) * S],
                          xT.ap()[cc * 128:(cc + 1) * 128, :])

        state = {
            "wslab_reuse": [0, 0],
            "ps_guard": [0, 0, 0],
        }

        def qk_tile(dt, is_k):
            wsrc = (wk_t if is_k else wq_t).ap()[dt]
            vw = [0, 0]
            for half in range(2):
                slot = half
                ssem = "g_wsl%d" % slot
                if state["wslab_reuse"][slot] > 0:
                    c.wait("gpsimd", "mm", state["wslab_reuse"][slot])
                src = wsrc[:, 16 * half:16 * (half + 1), :].rearrange("p a b -> p (a b)")
                vw[half] = c.dma("gpsimd", ssem, wslab[slot][:], src)
            c.wait("tensor", "g_wsl0", vw[0])
            c.wait("tensor", "g_wsl1", vw[1])
            c.wait("tensor", "g_xbt", v_xbt)
            last_mm = 0
            last_ve = 0
            for sb in range(4):
                bank = (dt * 4 + sb) % 3
                ps = qk_ps[bank]
                if state["ps_guard"][bank] > 0:
                    c.wait("tensor", "ve", state["ps_guard"][bank])
                ins = None
                if c.on("tensor"):
                    for cc in range(32):
                        ins = nc_.tensor.matmul(
                            ps[:], wslab[cc // 16][:, (cc % 16) * 128:(cc % 16 + 1) * 128],
                            xbT[:, cc * S + sb * 512: cc * S + (sb + 1) * 512],
                            start=(cc == 0), stop=(cc == 31))
                mm_v = c.inc(ins, "mm")
                last_mm = mm_v
                c.wait("vector", "mm", mm_v)
                c.wait("vector", "ld_s", max(v_cos, v_sin))
                cs = slice(sb * 512, (sb + 1) * 512)
                if is_k:
                    dst = KT[:, dt * S + sb * 512: dt * S + (sb + 1) * 512]
                else:
                    dst = qstage[dt % 2][:, cs]
                ins = None
                if c.on("vector"):
                    nc_.vector.tensor_mul(rt_t2[0:64, :], ps[64:128, :], sinb[0:64, cs])
                    nc_.vector.tensor_mul(rt_t2[64:128, :], ps[0:64, :], sinb[64:128, cs])
                    nc_.vector.tensor_mul(rt_t1[:], ps[:], cosb[:, cs])
                    ins = nc_.vector.tensor_add(dst, rt_t1[:], rt_t2[:])
                last_ve = c.inc(ins, "ve")
                state["ps_guard"][bank] = last_ve
            state["wslab_reuse"][0] = last_mm
            state["wslab_reuse"][1] = last_mm
            return last_ve

        qspill = [0, 0]
        for dt in range(NH):
            if qspill[dt % 2] > 0:
                c.wait("vector", "sp", qspill[dt % 2])
            last_rope = qk_tile(dt, is_k=False)
            c.wait("sync", "ve", last_rope)
            qspill[dt % 2] = c.dma("sync", "sp", qt_s.ap()[dt], qstage[dt % 2][:])
        for kt in range(NKV):
            qk_tile(kt, is_k=True)

        c.barrier()

        vsp = [0, 0]
        c.wait("gpsimd", "sp", max(qspill))
        vslab_reuse = 0
        for dvq in range(4):
            if vslab_reuse > 0:
                c.wait("gpsimd", "mm", vslab_reuse)
            vw = c.dma("gpsimd", "g_vsl", vslab[:],
                       wv_t.ap()[dvq].rearrange("p a b -> p (a b)"))
            c.wait("tensor", "g_vsl", vw)
            for st in range(16):
                bank = st % 3
                ps = qk_ps[bank]
                if state["ps_guard"][bank] > 0:
                    c.wait("tensor", "ve", state["ps_guard"][bank])
                ins = None
                if c.on("tensor"):
                    for cc in range(32):
                        ins = nc_.tensor.matmul(
                            ps[:, 0:256],
                            xbT[:, cc * S + st * 128: cc * S + st * 128 + 128],
                            vslab[:, cc * 256:(cc + 1) * 256],
                            start=(cc == 0), stop=(cc == 31))
                mm_v = c.inc(ins, "mm")
                vslab_reuse = mm_v
                if vsp[st % 2] > 0:
                    c.wait("vector", "sp", vsp[st % 2])
                c.wait("vector", "mm", mm_v)
                ins = None
                if c.on("vector"):
                    ins = nc_.vector.tensor_copy(vstage[st % 2][:], ps[:, 0:256])
                ve_v = c.inc(ins, "ve")
                state["ps_guard"][bank] = ve_v
                c.wait("sync", "ve", ve_v)
                dst = v_s.ap()[2 * dvq:2 * dvq + 2, st].rearrange("g p d -> p g d")
                src3 = vstage[st % 2][:].rearrange("p (g d) -> p g d", g=2)
                vsp[st % 2] = c.dma("sync", "sp", dst, src3)

        if dbg_sdb1 is not None:
            c.dma("sync", "sp", dbg_sdb1.ap(), sdb[:])
        c.barrier()

        vg_guard = [0, 0]
        qh_guard = [0, 0]
        pt_guard = [0, 0, 0, 0]
        stb_ac = [0, 0]
        stb_ve = [0, 0]
        ot_guard = [0, 0]
        den_guard = [0, 0]
        bc_guard = 0
        woq_guard = [0] * 5
        cst_guard = [0, 0, 0, 0]
        last_vgl = 0
        x1sp = 0
        ybuf_ac_guard = 0

        for qb in range(4):
            last_exp = 0
            last_ve_head = 0
            for h in range(NH):
                g = h // 4
                if h % 4 == 0:
                    if vg_guard[g % 2] > 0:
                        c.wait("sync", "mm", vg_guard[g % 2])
                    last_vgl = c.dma("sync", "ld_s",
                                     vg[g % 2][:].rearrange("p (t d) -> p t d", t=16),
                                     v_s.ap()[g].rearrange("t p d -> p t d"))
                if qh_guard[h % 2] > 0:
                    c.wait("sync", "mm", qh_guard[h % 2])
                qhl = c.dma("sync", "ld_s", qh[h % 2][:],
                            qt_s.ap()[h][:, qb * 512:(qb + 1) * 512])
                nkt = 4 * qb + 4
                c.wait("tensor", "ld_s", max(qhl, last_vgl))
                pv_v = 0
                for kt in range(nkt):
                    j = kt - 4 * qb
                    vis0 = 128 * max(j, 0)
                    bank = kt % 2
                    spn = st_ps[bank]
                    if stb_ac[bank] > 0:
                        c.wait("tensor", "ac", stb_ac[bank])
                    if stb_ve[bank] > 0:
                        c.wait("tensor", "ve", stb_ve[bank])
                    ins = None
                    if c.on("tensor"):
                        ins = nc_.tensor.matmul(
                            spn[:, vis0:512],
                            KT[:, g * S + kt * 128: g * S + kt * 128 + 128],
                            qh[h % 2][:, vis0:512], start=True, stop=True)
                    mm_v = c.inc(ins, "mm")
                    ptn = pt[kt % 4]
                    if pt_guard[kt % 4] > 0:
                        c.wait("scalar", "mm", pt_guard[kt % 4])
                    c.wait("scalar", "mm", mm_v)
                    c.wait("scalar", "ve", v_msets)
                    ins = None
                    if c.on("scalar"):
                        ins = nc_.scalar.activation(
                            ptn[:, vis0:512], spn[:, vis0:512], AF.Exp,
                            bias=negB[:], scale=SCALE)
                    ac_v = c.inc(ins, "ac")
                    stb_ac[bank] = ac_v
                    last_exp = ac_v
                    if j >= 0:
                        c.wait("vector", "ac", ac_v)
                        c.wait("vector", "ld_s", v_tri)
                        ins = None
                        if c.on("vector"):
                            ins = nc_.vector.tensor_mul(
                                ptn[:, vis0:vis0 + 128],
                                ptn[:, vis0:vis0 + 128], trib[:])
                        mask_v = c.inc(ins, "ve")
                        c.wait("tensor", "ve", mask_v)
                    else:
                        c.wait("tensor", "ac", ac_v)
                    c.wait("tensor", "ld_s", v_onc)
                    if kt == 0:
                        if den_guard[h % 2] > 0:
                            c.wait("tensor", "ve", den_guard[h % 2])
                        if ot_guard[h % 2] > 0:
                            c.wait("tensor", "ve", ot_guard[h % 2])
                    ins = None
                    if c.on("tensor"):
                        nc_.tensor.matmul(
                            den_ps[h % 2][:, vis0:512], oncb[:], ptn[:, vis0:512],
                            start=(kt == 0), stop=(kt == nkt - 1))
                        ins = nc_.tensor.matmul(
                            ot_ps[h % 2][:, vis0:512],
                            vg[g % 2][:, kt * 128:kt * 128 + 128],
                            ptn[:, vis0:512],
                            start=(kt == 0), stop=(kt == nkt - 1))
                    mm_v = c.inc(ins, "mm")
                    pt_guard[kt % 4] = mm_v
                    pv_v = mm_v
                qh_guard[h % 2] = pv_v
                if h % 4 == 3:
                    vg_guard[g % 2] = pv_v
                c.wait("vector", "mm", pv_v)
                ins = None
                if c.on("vector"):
                    with nc_.allow_low_precision("1/den rounded to bf16; within the bf16 operand precision class"):
                        ins = nc_.vector.reciprocal(recip[:], den_ps[h % 2][:])
                ve_v = c.inc(ins, "ve")
                den_guard[h % 2] = ve_v
                c.wait("vector", "ve", ve_v)
                c.wait("tensor", "ve", ve_v)
                if bc_guard > 0:
                    c.wait("tensor", "ve", bc_guard)
                ins = None
                if c.on("tensor"):
                    ins = nc_.tensor.matmul(bc_ps[:], onrb[:], recip[:],
                                            start=True, stop=True)
                mm_v = c.inc(ins, "mm")
                c.wait("vector", "mm", mm_v)
                ins = None
                if c.on("vector"):
                    ins = nc_.vector.tensor_copy(recipb[:], bc_ps[:])
                bc_guard = c.inc(ins, "ve")
                ins = None
                if c.on("vector"):
                    ins = nc_.vector.tensor_mul(
                        OTsl[:, h * 512:(h + 1) * 512], ot_ps[h % 2][:], recipb[:])
                last_ve_head = c.inc(ins, "ve")
                ot_guard[h % 2] = last_ve_head

            c.wait("tensor", "ac", last_exp)
            c.wait("tensor", "ve", last_ve_head)
            for cb in range(8):
                qvals = []
                for q in range(4):
                    slot = (cb * 4 + q) % 5
                    ssem = "g_wo%d" % slot
                    if woq_guard[slot] > 0:
                        c.wait("gpsimd", "mm", woq_guard[slot])
                    vq = c.dma("gpsimd", ssem, woq[slot][:],
                               wo_t.ap()[cb][:, q * 8:(q + 1) * 8, :]
                               .rearrange("p a b -> p (a b)"))
                    qvals.append((slot, vq))
                for slot, vq in qvals:
                    c.wait("tensor", "g_wo%d" % slot, vq)
                for st in range(4):
                    bank = st % 2
                    ps = st_ps[bank]
                    if stb_ve[bank] > 0:
                        c.wait("tensor", "ve", stb_ve[bank])
                    if cst_guard[st % 4] > 0:
                        c.wait("vector", "sp", cst_guard[st % 4])
                    ins = None
                    if c.on("tensor"):
                        for cc in range(32):
                            slot = (cb * 4 + cc // 8) % 5
                            ins = nc_.tensor.matmul(
                                ps[:],
                                OTsl[:, cc * 512 + st * 128: cc * 512 + st * 128 + 128],
                                woq[slot][:, (cc % 8) * 512:(cc % 8 + 1) * 512],
                                start=(cc == 0), stop=(cc == 31))
                    mm_v = c.inc(ins, "mm")
                    for q in range(4):
                        woq_guard[(cb * 4 + q) % 5] = mm_v
                    c.wait("vector", "mm", mm_v)
                    ins = None
                    if c.on("vector"):
                        ins = nc_.vector.tensor_copy(cstage[st % 4][:], ps[:])
                    ve_v = c.inc(ins, "ve")
                    stb_ve[bank] = ve_v
                    c.wait("sync", "ve", ve_v)
                    rows = qb * 512 + st * 128
                    cst_guard[st % 4] = c.dma(
                        "sync", "sp",
                        y_s.ap()[rows:rows + 128, cb * 512:(cb + 1) * 512],
                        cstage[st % 4][:])
            all_y = cst_guard[3]

            for st in range(4):
                rows = qb * 512 + st * 128
                c.wait("sync", "sp", max(all_y, x1sp))
                if ybuf_ac_guard > 0:
                    c.wait("sync", "ac", ybuf_ac_guard)
                vy = c.dma("sync", "ld_s", ybuf[:], y_s.ap()[rows:rows + 128, :])
                vx = c.dma("sync", "ld_s", xzbuf[:], x_nat.ap()[rows:rows + 128, :])
                c.wait("vector", "ld_s", vx)
                ins = None
                if c.on("vector"):
                    nc_.vector.tensor_add(xzbuf[:], xzbuf[:], ybuf[:])
                    ins = nc_.vector.reduce_sum(sm["s1"][:], xzbuf[:], AX.X)
                ve_z = c.inc(ins, "ve")
                c.wait("vector", "ve", ve_z)
                c.wait("scalar", "ve", ve_z)
                c.wait("scalar", "sp", x1sp)
                ins = None
                if c.on("scalar"):
                    ins = nc_.scalar.activation(ybuf[:], xzbuf[:], AF.Square,
                                                accum_out=sm["s2"][:])
                ac_sq = c.inc(ins, "ac")
                ybuf_ac_guard = ac_sq
                ins = None
                if c.on("vector"):
                    ins = nc_.vector.tensor_scalar_mul(sm["mean"][:], sm["s1"][:], 1.0 / H)
                ve_mean = c.inc(ins, "ve")
                c.wait("vector", "ve", ve_mean)
                c.wait("scalar", "ve", ve_mean)
                ins = None
                if c.on("scalar"):
                    ins = nc_.scalar.activation(sm["m2"][:], sm["mean"][:], AF.Square)
                ac_m2 = c.inc(ins, "ac")
                c.wait("scalar", "ac", ac_m2)
                c.wait("vector", "ac", max(ac_sq, ac_m2))
                ins = None
                if c.on("vector"):
                    ins = nc_.vector.scalar_tensor_tensor(
                        sm["var"][:], sm["s2"][:], 1.0 / H, sm["m2"][:],
                        ALU.mult, ALU.subtract)
                ve_var = c.inc(ins, "ve")
                c.wait("vector", "ve", ve_var)
                c.wait("scalar", "ve", ve_var)
                ins = None
                if c.on("scalar"):
                    nc_.scalar.activation(sm["lnv"][:], sm["var"][:], AF.Ln, bias=epsb[:])
                    ins = nc_.scalar.activation(sm["rstd"][:], sm["lnv"][:], AF.Exp,
                                                scale=-0.5)
                ac_rstd = c.inc(ins, "ac")
                c.wait("scalar", "ac", ac_rstd)
                c.wait("vector", "ac", ac_rstd)
                ins = None
                if c.on("vector"):
                    ins = nc_.vector.scalar_tensor_tensor(
                        sm["nmr"][:], sm["mean"][:], -1.0, sm["rstd"][:],
                        ALU.mult, ALU.mult)
                ve_nmr = c.inc(ins, "ve")
                c.wait("vector", "ve", ve_nmr)
                c.wait("scalar", "ve", ve_nmr)
                ins = None
                if c.on("scalar"):
                    ins = nc_.scalar.activation(x1buf[:], xzbuf[:], AF.Identity,
                                                scale=sm["rstd"][:], bias=sm["nmr"][:])
                ac_x1 = c.inc(ins, "ac")
                c.wait("sync", "ac", ac_x1)
                x1sp = c.dma("sync", "sp", x1_s.ap()[rows:rows + 128, :], x1buf[:])

        c.barrier()

        x1t_guard = 0
        wg_guard = [0, 0]
        wd_guard = [0, 0]
        h_guard = 0
        gate_guard = 0
        up_guard = 0
        dn_guard = [0, 0]
        out_guard = 0
        out_sp = 0
        x1re_guard = 0
        x1re_ac_guard = 0
        ic_groups = [(0, 11), (11, 22), (22, 33), (33, 44),
                     (44, 55), (55, 66), (66, 77), (77, 86)]
        for ch in range(NCH):
            if os.environ.get("KSER_BAR"):
                c.barrier()
            if x1t_guard > 0:
                c.wait("sync", "mm", x1t_guard)
            vxt = 0
            for cc in range(32):
                vxt = c.dma("sync", "ld_s",
                            x1T[:, cc * SB:(cc + 1) * SB],
                            x1_s.ap()[ch * SB:(ch + 1) * SB, cc * 128:(cc + 1) * 128],
                            transpose=True)
            mm_u = 0
            ve_h = 0
            for it in range(NIT):
                slot = it % 2
                ssem = "g_wg%d" % slot
                if wg_guard[slot] > 0:
                    c.wait("gpsimd", "mm", wg_guard[slot])
                c.dma("gpsimd", ssem, wgr[slot][:],
                      wg_t.ap()[it].rearrange("p a b -> p (a b)"))
                vu_w = c.dma("gpsimd", ssem, wur[slot][:],
                             wu_t.ap()[it].rearrange("p a b -> p (a b)"))
                c.wait("tensor", ssem, vu_w)
                c.wait("tensor", "ld_s", vxt)
                if it == 0 and h_guard > 0:
                    c.wait("tensor", "mm", h_guard)
                if gate_guard > 0:
                    c.wait("tensor", "ac", gate_guard)
                ins = None
                if c.on("tensor"):
                    for cc in range(32):
                        ins = nc_.tensor.matmul(
                            gate_ps[:], wgr[slot][:, cc * 128:(cc + 1) * 128],
                            x1T[:, cc * SB:(cc + 1) * SB],
                            start=(cc == 0), stop=(cc == 31))
                mm_g = c.inc(ins, "mm")
                if up_guard > 0:
                    c.wait("tensor", "ve", up_guard)
                ins = None
                if c.on("tensor"):
                    for cc in range(32):
                        ins = nc_.tensor.matmul(
                            up_ps[:], wur[slot][:, cc * 128:(cc + 1) * 128],
                            x1T[:, cc * SB:(cc + 1) * SB],
                            start=(cc == 0), stop=(cc == 31))
                mm_u = c.inc(ins, "mm")
                wg_guard[slot] = mm_u
                c.wait("scalar", "mm", mm_g)
                c.wait("scalar", "ld_s", v_sg)
                if up_guard > 0:
                    c.wait("scalar", "ve", up_guard)
                ins = None
                if c.on("scalar"):
                    ins = nc_.scalar.activation(t1buf[:], gate_ps[:], AF.Silu,
                                                scale=sgb[:, it:it + 1])
                ac_t1 = c.inc(ins, "ac")
                gate_guard = ac_t1
                c.wait("vector", "mm", mm_u)
                c.wait("vector", "ac", ac_t1)
                c.wait("vector", "ld_s", v_su)
                ins = None
                if c.on("vector"):
                    ins = nc_.vector.scalar_tensor_tensor(
                        hstore[:, it * SB:(it + 1) * SB], up_ps[:],
                        sub[:, it:it + 1], t1buf[:], ALU.mult, ALU.mult)
                ve_h = c.inc(ins, "ve")
                up_guard = ve_h
            x1t_guard = mm_u
            if os.environ.get("KSER_BAR"):
                c.barrier()
            if dbg_h2 is not None and ch == 0:
                c.wait("sync", "ve", ve_h)
                c.dma("sync", "sp", dbg_h2.ap(), hstore[:])
                c.dma("sync", "sp", dbg_sdb.ap(), sdb[:])
            c.wait("tensor", "ve", ve_h)
            mm_dn = 0
            for ob in range(8):
                for gi, (ic0, ic1) in enumerate(ic_groups):
                    slot = (ob * 8 + gi) % 2
                    ssem = "g_wd%d" % slot
                    if wd_guard[slot] > 0:
                        c.wait("gpsimd", "mm", wd_guard[slot])
                    if os.environ.get("KSER_WD"):
                        c.wait("gpsimd", "mm", c.cnt["mm"])
                    ni = ic1 - ic0
                    vd_w = c.dma("gpsimd", ssem,
                                 wdring[slot][:, 0:ni * 512].rearrange(
                                     "p (i n) -> p i n", i=ni),
                                 wd_t.ap()[ob][ic0:ic1].rearrange("i p n -> p i n"))
                    c.wait("tensor", ssem, vd_w)
                    if gi == 0:
                        for st in range(2):
                            if dn_guard[st] > 0:
                                c.wait("tensor", "ve", dn_guard[st])
                    ins = None
                    if c.on("tensor"):
                        for ic in range(ic0, ic1):
                            for st in range(2):
                                ins = nc_.tensor.matmul(
                                    dn_ps[st][:],
                                    hstore[:, ic * SB + st * 128: ic * SB + st * 128 + 128],
                                    wdring[slot][:, (ic - ic0) * 512:(ic - ic0 + 1) * 512],
                                    start=(ic == 0), stop=(ic == NIT - 1))
                    mm_dn = c.inc(ins, "mm")
                    if os.environ.get("KSER_DN"):
                        c.wait("tensor", "mm", mm_dn)
                    wd_guard[slot] = mm_dn
                    if dbg_wd is not None and ch == 0 and ob == 0 and gi < 2:
                        c.wait("sync", "mm", mm_dn)
                        wddump = c.dma("sync", "sp", dbg_wd.ap()[gi], wdring[slot][:])
                        c.wait("gpsimd", "sp", wddump)
                for st in range(2):
                    c.wait("vector", "mm", mm_dn)
                    if dbg_dn is not None and ch == 0 and ob == 0:
                        ins = None
                        if c.on("vector"):
                            ins = nc_.vector.tensor_copy(zbuf[:, st * 512:(st + 1) * 512],
                                                         dn_ps[st][:])
                        dvv = c.inc(ins, "ve")
                        c.wait("sync", "ve", dvv)
                        ddump = c.dma("sync", "sp", dbg_dn.ap()[st],
                                      zbuf[:, st * 512:(st + 1) * 512])
                        c.wait("vector", "sp", ddump)
                    ins = None
                    if c.on("vector"):
                        ins = nc_.vector.tensor_mul(
                            yst[:, st * H + ob * 512: st * H + (ob + 1) * 512],
                            dn_ps[st][:], sdb[:, ob * 512:(ob + 1) * 512])
                    ve_v = c.inc(ins, "ve")
                    dn_guard[st] = ve_v
            y2_done = max(dn_guard)
            h_guard = mm_dn
            if os.environ.get("KSER_BAR"):
                c.barrier()
            if dbg_x1t is not None and ch == 0:
                c.wait("sync", "ve", y2_done)
                c.dma("sync", "sp", dbg_x1t.ap(), x1T[:])
                c.dma("sync", "sp", dbg_h.ap(), hstore[:])
                c.dma("sync", "sp", dbg_y2.ap(), yst[:])

            for st in range(2):
                rows = ch * SB + st * 128
                if x1re_guard > 0:
                    c.wait("sync", "ve", x1re_guard)
                if x1re_ac_guard > 0:
                    c.wait("sync", "ac", x1re_ac_guard)
                vx1 = c.dma("sync", "ld_s", x1re[:], x1_s.ap()[rows:rows + 128, :])
                c.wait("vector", "ld_s", vx1)
                c.wait("vector", "ve", y2_done)
                if out_guard > 0:
                    c.wait("vector", "ac", out_guard)
                ins = None
                if c.on("vector"):
                    nc_.vector.tensor_add(zbuf[:], x1re[:], yst[:, st * H:(st + 1) * H])
                    ins = nc_.vector.reduce_sum(sm2["s1"][:], zbuf[:], AX.X)
                ve_z = c.inc(ins, "ve")
                x1re_guard = ve_z
                c.wait("vector", "ve", ve_z)
                c.wait("scalar", "ve", ve_z)
                ins = None
                if c.on("scalar"):
                    ins = nc_.scalar.activation(x1re[:], zbuf[:], AF.Square,
                                                accum_out=sm2["s2"][:])
                ac_sq = c.inc(ins, "ac")
                x1re_ac_guard = ac_sq
                ins = None
                if c.on("vector"):
                    ins = nc_.vector.tensor_scalar_mul(sm2["mean"][:], sm2["s1"][:], 1.0 / H)
                ve_mean = c.inc(ins, "ve")
                c.wait("vector", "ve", ve_mean)
                c.wait("scalar", "ve", ve_mean)
                ins = None
                if c.on("scalar"):
                    ins = nc_.scalar.activation(sm2["m2"][:], sm2["mean"][:], AF.Square)
                ac_m2 = c.inc(ins, "ac")
                c.wait("scalar", "ac", ac_m2)
                c.wait("vector", "ac", max(ac_sq, ac_m2))
                ins = None
                if c.on("vector"):
                    ins = nc_.vector.scalar_tensor_tensor(
                        sm2["var"][:], sm2["s2"][:], 1.0 / H, sm2["m2"][:],
                        ALU.mult, ALU.subtract)
                ve_var = c.inc(ins, "ve")
                c.wait("vector", "ve", ve_var)
                c.wait("scalar", "ve", ve_var)
                ins = None
                if c.on("scalar"):
                    nc_.scalar.activation(sm2["lnv"][:], sm2["var"][:], AF.Ln, bias=epsb[:])
                    ins = nc_.scalar.activation(sm2["rstd"][:], sm2["lnv"][:], AF.Exp,
                                                scale=-0.5)
                ac_rstd = c.inc(ins, "ac")
                c.wait("scalar", "ac", ac_rstd)
                c.wait("vector", "ac", ac_rstd)
                ins = None
                if c.on("vector"):
                    ins = nc_.vector.scalar_tensor_tensor(
                        sm2["nmr"][:], sm2["mean"][:], -1.0, sm2["rstd"][:],
                        ALU.mult, ALU.mult)
                ve_nmr = c.inc(ins, "ve")
                c.wait("vector", "ve", ve_nmr)
                c.wait("scalar", "ve", ve_nmr)
                for hf in range(4):
                    if out_guard > 0:
                        c.wait("scalar", "so", out_sp)
                    ins = None
                    if c.on("scalar"):
                        ins = nc_.scalar.activation(
                            outst[:], zbuf[:, hf * 1024:(hf + 1) * 1024], AF.Identity,
                            scale=sm2["rstd"][:], bias=sm2["nmr"][:])
                    ac_o = c.inc(ins, "ac")
                    out_guard = ac_o
                    c.wait("sync", "ac", ac_o)
                    out_sp = c.dma("sync", "so",
                                   out.ap()[rows:rows + 128, hf * 1024:(hf + 1) * 1024],
                                   outst[:])

        c.wait("sync", "so", c.cnt["so"])
        c.barrier()

    return nc, sems, emit


def _round_bf16(a):
    return np.ascontiguousarray(a).astype(BFNP)


def _host_prep(inputs):
    X = np.asarray(inputs["L0_attn_hidden_states"], dtype=np.float32)
    Wq = np.asarray(inputs["L0_attn_Wq"], dtype=np.float32)
    Wk = np.asarray(inputs["L0_attn_Wk"], dtype=np.float32)
    Wv = np.asarray(inputs["L0_attn_Wv"], dtype=np.float32)
    Wo = np.asarray(inputs["L0_attn_Wo"], dtype=np.float32)
    wg = np.asarray(inputs["L0_ffn_w_gate"])
    sg = np.asarray(inputs["L0_ffn_s_gate"], dtype=np.float32)
    wu = np.asarray(inputs["L0_ffn_w_up"])
    su = np.asarray(inputs["L0_ffn_s_up"], dtype=np.float32)
    wd = np.asarray(inputs["L0_ffn_w_down"])
    sd = np.asarray(inputs["L0_ffn_s_down"], dtype=np.float32)

    sh = {}
    sh["wq_t"] = np.ascontiguousarray(Wq.T.reshape(32, 128, 32, 128).transpose(2, 1, 0, 3))
    sh["wk_t"] = np.ascontiguousarray(Wk.T.reshape(32, 128, 8, 128).transpose(2, 1, 0, 3))
    sh["wv_t"] = np.ascontiguousarray(Wv.T.reshape(32, 128, 4, 256).transpose(2, 1, 0, 3))
    sh["wo_t"] = np.ascontiguousarray(Wo.T.reshape(32, 128, 8, 512).transpose(2, 1, 0, 3))
    sh["wg_t"] = np.ascontiguousarray(wg.T.reshape(32, 128, NIT, 128).transpose(2, 1, 0, 3))
    sh["wu_t"] = np.ascontiguousarray(wu.T.reshape(32, 128, NIT, 128).transpose(2, 1, 0, 3))
    sh["wd_t"] = np.ascontiguousarray(wd.T.reshape(NIT, 128, 8, 512).transpose(2, 0, 1, 3))
    sh["sg_t"] = np.ascontiguousarray(sg.reshape(NIT, 128).T)
    sh["su_t"] = np.ascontiguousarray(su.reshape(NIT, 128).T)
    sh["sdb_in"] = np.ascontiguousarray(np.broadcast_to(sd.reshape(1, H), (128, H)))

    inv = 1.0 / 10000.0 ** (np.arange(0, D, 2, dtype=np.float32) / D)
    t = np.arange(S, dtype=np.float32)
    fr = np.outer(t, inv)
    emb = np.concatenate([fr, fr], axis=-1)
    sh["cos_t"] = _round_bf16(np.cos(emb).astype(np.float32).T)
    sin2T = np.sin(emb).astype(np.float32).T.copy()
    sin2T[0:64, :] *= -1.0
    sh["sin_t"] = _round_bf16(sin2T)
    sh["tri_t"] = _round_bf16(np.triu(np.ones((128, 128), dtype=np.float32)))
    sh["onc_t"] = _round_bf16(np.ones((128, 1), dtype=np.float32))
    sh["onr_t"] = np.ones((1, 128), dtype=BFNP)

    in_maps = []
    for b in range(8):
        m = dict(sh)
        m["x_nat"] = np.ascontiguousarray(X[b])
        m["xT"] = np.ascontiguousarray(X[b].T)
        in_maps.append(m)
    return in_maps


def kernel(**inputs):
    in_maps = _host_prep(inputs)
    nc, sems, emit = build_program()
    with nc.Block() as block:
        for name in ENGINES:
            getattr(block, name)(lambda e, n=name: emit(Ctx(nc, n, sems)))
    res = run_bass_kernel_spmd(nc, in_maps, core_ids=list(range(8)))
    outs = [r["out"] for r in res.results]
    return np.stack(outs, axis=0).astype(np.float32)


# version 05
# speedup vs baseline: 1.3262x; 1.3262x over previous
import os
import numpy as np
import ml_dtypes
import concourse.bass as bass
import concourse.mybir as mybir
from concourse.bass_utils import run_bass_kernel_spmd

F32 = mybir.dt.float32
BF16 = mybir.dt.bfloat16
I8 = mybir.dt.int8
AF = mybir.ActivationFunctionType
ALU = mybir.AluOpType
AX = mybir.AxisListType
BFNP = ml_dtypes.bfloat16

S = 2048
H = 4096
D = 128
NH = 32
NKV = 8
INTER = 11008
NIT = INTER // 128
SB = 256
NCH = S // SB
EXP_SHIFT = 25.0
SCALE = 1.0 / float(np.sqrt(np.float32(D)))

ENGINES = ("tensor", "vector", "scalar", "sync", "gpsimd")

SEM_NAMES = [
    "mm", "ve", "ac",
    "ld_s", "sp", "so",
    "g_xbt",
    "g_wsl0", "g_wsl1",
    "g_vsl",
    "g_wo0", "g_wo1", "g_wo2", "g_wo3", "g_wo4",
    "g_wg0", "g_wg1",
    "g_wd0", "g_wd1",
    "bar",
]


class Ctx:
    def __init__(self, nc, eng_name, sems):
        self.nc = nc
        self.eng = eng_name
        self.e = {"tensor": nc.tensor, "vector": nc.vector, "scalar": nc.scalar,
                  "sync": nc.sync, "gpsimd": nc.gpsimd}[eng_name]
        self.sems = sems
        self.cnt = {n: 0 for n in sems}
        self.nbar = 0

    def on(self, name):
        return self.eng == name

    def inc(self, ins, sem, n=1):
        self.cnt[sem] += n
        if ins is not None:
            ins.then_inc(self.sems[sem], n)
        return self.cnt[sem]

    def wait(self, owner, sem, val):
        if self.eng == owner and val > 0:
            self.e.wait_ge(self.sems[sem], val)

    def dma(self, owner, sem, dst, src, transpose=False):
        ins = None
        if self.on(owner):
            eng = {"sync": self.nc.sync, "gpsimd": self.nc.gpsimd}[owner]
            if transpose:
                ins = eng.dma_start(out=dst, in_=src, transpose=True)
            else:
                ins = eng.dma_start(out=dst, in_=src)
        return self.inc(ins, sem, 16)

    def barrier(self):
        self.nbar += 1
        self.cnt["bar"] += 5
        self.e.sem_inc(self.sems["bar"], 1)
        self.e.wait_ge(self.sems["bar"], 5 * self.nbar)


def build_program(debug_scratch=False):
    nc = bass.Bass("TRN2", target_bir_lowering=False, debug=False)
    SK = "ExternalOutput" if debug_scratch else "Internal"

    x_nat = nc.dram_tensor("x_nat", [S, H], F32, kind="ExternalInput")
    xT = nc.dram_tensor("xT", [H, S], F32, kind="ExternalInput")
    wq_t = nc.dram_tensor("wq_t", [32, 128, 32, 128], F32, kind="ExternalInput")
    wk_t = nc.dram_tensor("wk_t", [8, 128, 32, 128], F32, kind="ExternalInput")
    wv_t = nc.dram_tensor("wv_t", [4, 128, 32, 256], F32, kind="ExternalInput")
    wo_t = nc.dram_tensor("wo_t", [8 * 4 * 128 * 4096], F32, kind="ExternalInput")
    wg_t = nc.dram_tensor("wg_t", [NIT, 128, 32, 128], I8, kind="ExternalInput")
    wu_t = nc.dram_tensor("wu_t", [NIT, 128, 32, 128], I8, kind="ExternalInput")
    wd_t = nc.dram_tensor("wd_t", [8 * NIT * 128 * 512], I8, kind="ExternalInput")
    sg_t = nc.dram_tensor("sg_t", [128, NIT], F32, kind="ExternalInput")
    su_t = nc.dram_tensor("su_t", [128, NIT], F32, kind="ExternalInput")
    sdb_in = nc.dram_tensor("sdb_in", [128, H], F32, kind="ExternalInput")
    cos_t = nc.dram_tensor("cos_t", [128, S], BF16, kind="ExternalInput")
    sin_t = nc.dram_tensor("sin_t", [128, S], BF16, kind="ExternalInput")
    tri_t = nc.dram_tensor("tri_t", [128, 128], BF16, kind="ExternalInput")
    onc_t = nc.dram_tensor("onc_t", [128, 1], BF16, kind="ExternalInput")
    onr_t = nc.dram_tensor("onr_t", [1, 128], BF16, kind="ExternalInput")

    out = nc.dram_tensor("out", [S, H], F32, kind="ExternalOutput")

    qt_s = nc.dram_tensor("qt_s", [NH, 128, S], BF16, kind=SK)
    v_s = nc.dram_tensor("v_s", [NKV, 16, 128, 128], BF16, kind=SK)
    y_s = nc.dram_tensor("y_s", [S, H], BF16, kind=SK)
    x1_s = nc.dram_tensor("x1_s", [S, H], BF16, kind=SK)
    if debug_scratch:
        dbg_x1t = nc.dram_tensor("dbg_x1t", [128, 32 * SB], BF16, kind="ExternalOutput")
        dbg_h = nc.dram_tensor("dbg_h", [128, NIT * SB], BF16, kind="ExternalOutput")
        dbg_y2 = nc.dram_tensor("dbg_y2", [128, 2 * H], BF16, kind="ExternalOutput")
        dbg_wd = nc.dram_tensor("dbg_wd", [2, 128, 11 * 512], BF16, kind="ExternalOutput")
        dbg_h2 = nc.dram_tensor("dbg_h2", [128, NIT * SB], BF16, kind="ExternalOutput")
        dbg_sdb = nc.dram_tensor("dbg_sdb", [128, H], F32, kind="ExternalOutput")
        dbg_sdb0 = nc.dram_tensor("dbg_sdb0", [128, H], F32, kind="ExternalOutput")
        dbg_sdb1 = nc.dram_tensor("dbg_sdb1", [128, H], F32, kind="ExternalOutput")
        dbg_dn = nc.dram_tensor("dbg_dn", [2, 128, 512], F32, kind="ExternalOutput")
    else:
        dbg_x1t = dbg_h = dbg_y2 = dbg_wd = dbg_h2 = dbg_sdb = dbg_dn = None
        dbg_sdb0 = dbg_sdb1 = None

    KT = nc.alloc_sbuf_tensor("KT", [128, NKV * S], BF16)
    sgb = nc.alloc_sbuf_tensor("sgb", [128, NIT], F32)
    sub = nc.alloc_sbuf_tensor("sub", [128, NIT], F32)
    sdb = nc.alloc_sbuf_tensor("sdb", [128, H], F32)
    trib = nc.alloc_sbuf_tensor("trib", [128, 128], BF16)
    oncb = nc.alloc_sbuf_tensor("oncb", [128, 1], BF16)
    onrb = nc.alloc_sbuf_tensor("onrb", [1, 128], BF16)
    negB = nc.alloc_sbuf_tensor("negB", [128, 1], F32)
    epsb = nc.alloc_sbuf_tensor("epsb", [128, 1], F32)

    mark = nc.sbuf_base

    xbT = nc.alloc_sbuf_tensor("xbT", [128, 32 * S], BF16)
    a_sub = nc.sbuf_base
    cosb = nc.alloc_sbuf_tensor("cosb", [128, S], BF16)
    sinb = nc.alloc_sbuf_tensor("sinb", [128, S], BF16)
    wslab = [nc.alloc_sbuf_tensor(f"wslab{i}", [128, 2048], BF16) for i in range(2)]
    qstage = [nc.alloc_sbuf_tensor(f"qstage{i}", [128, S], BF16) for i in range(2)]
    rt_t1 = nc.alloc_sbuf_tensor("rt_t1", [128, 512], BF16)
    rt_t2 = nc.alloc_sbuf_tensor("rt_t2", [128, 512], BF16)
    a_peak = nc.sbuf_base
    nc.sbuf_base = a_sub
    vslab = nc.alloc_sbuf_tensor("vslab", [128, 32 * 256], BF16)
    vstage = [nc.alloc_sbuf_tensor(f"vstage{i}", [128, 256], BF16) for i in range(2)]
    a_peak = max(a_peak, nc.sbuf_base)
    nc.sbuf_base = mark

    OTsl = nc.alloc_sbuf_tensor("OTsl", [128, NH * 512], BF16)
    qh = [nc.alloc_sbuf_tensor(f"qh{i}", [128, 512], BF16) for i in range(2)]
    vg = [nc.alloc_sbuf_tensor(f"vg{i}", [128, 16 * 128], BF16) for i in range(2)]
    pt = [nc.alloc_sbuf_tensor(f"pt{i}", [128, 512], BF16) for i in range(4)]
    woq = [nc.alloc_sbuf_tensor(f"woq{i}", [128, 8 * 512], BF16) for i in range(5)]
    cstage = [nc.alloc_sbuf_tensor(f"cstage{i}", [128, 512], BF16) for i in range(4)]
    recip = [nc.alloc_sbuf_tensor(f"recip{i}", [1, 512], BF16) for i in range(2)]
    recipb = nc.alloc_sbuf_tensor("recipb", [128, 512], F32)
    ybuf = nc.alloc_sbuf_tensor("ybuf", [128, H], BF16)
    xzbuf = nc.alloc_sbuf_tensor("xzbuf", [128, H], F32)
    x1buf = nc.alloc_sbuf_tensor("x1buf", [128, H], BF16)
    sm = {n: nc.alloc_sbuf_tensor(f"sm_{n}", [128, 1], F32)
          for n in ("s1", "s2", "mean", "m2", "var", "lnv", "rstd", "nmr")}
    bc_peak = nc.sbuf_base
    nc.sbuf_base = mark

    hstore = nc.alloc_sbuf_tensor("hstore", [128, NIT * SB], BF16)
    wdring = [nc.alloc_sbuf_tensor(f"wdring{i}", [128, 11 * 512], BF16) for i in range(2)]
    wgr = [nc.alloc_sbuf_tensor(f"wgr{i}", [128, 4096], BF16) for i in range(2)]
    wur = [nc.alloc_sbuf_tensor(f"wur{i}", [128, 4096], BF16) for i in range(2)]
    x1T = nc.alloc_sbuf_tensor("x1T", [128, 32 * SB], BF16, is_transpose_dest=True)
    t1buf = nc.alloc_sbuf_tensor("t1buf", [128, SB], BF16)
    yst = nc.alloc_sbuf_tensor("yst", [128, 2 * H], BF16)
    x1re = nc.alloc_sbuf_tensor("x1re", [128, H], BF16)
    zbuf = nc.alloc_sbuf_tensor("zbuf", [128, H], F32)
    outst = nc.alloc_sbuf_tensor("outst", [128, H // 4], F32)
    sm2 = {n: nc.alloc_sbuf_tensor(f"sm2_{n}", [128, 1], F32)
           for n in ("s1", "s2", "mean", "m2", "var", "lnv", "rstd", "nmr")}
    d_peak = nc.sbuf_base
    nc.sbuf_base = mark

    assert max(a_peak, bc_peak, d_peak) <= nc.sbuf_top, \
        (a_peak, bc_peak, d_peak, nc.sbuf_top)

    pmark = nc.psum_base
    qk_ps = [nc.alloc_psum_tensor(f"qk_ps{i}", [128, 512], F32) for i in range(3)]
    nc.psum_base = pmark
    st_ps = [nc.alloc_psum_tensor(f"st_ps{i}", [128, 512], F32) for i in range(2)]
    ot_ps = [nc.alloc_psum_tensor(f"ot_ps{i}", [128, 512], F32) for i in range(2)]
    den_ps = [nc.alloc_psum_tensor(f"den_ps{i}", [1, 512], F32) for i in range(2)]
    bc_ps = nc.alloc_psum_tensor("bc_ps", [128, 512], F32)
    nc.psum_base = pmark
    gate_ps = nc.alloc_psum_tensor("gate_ps", [128, SB], F32)
    up_ps = nc.alloc_psum_tensor("up_ps", [128, SB], F32)
    dn_ps = [nc.alloc_psum_tensor(f"dn_ps{i}", [128, 512], F32) for i in range(2)]
    nc.psum_base = pmark

    sems = {n: nc.alloc_semaphore(name=f"s_{n}") for n in SEM_NAMES}

    def emit(c: Ctx):
        nc_ = c.nc

        v_tri = c.dma("sync", "ld_s", trib[:], tri_t.ap())
        v_onc = c.dma("sync", "ld_s", oncb[:], onc_t.ap())
        v_onr = c.dma("sync", "ld_s", onrb[:], onr_t.ap())
        v_sg = c.dma("sync", "ld_s", sgb[:], sg_t.ap())
        v_su = c.dma("sync", "ld_s", sub[:], su_t.ap())
        v_sdr = c.dma("sync", "ld_s", sdb[:], sdb_in.ap())
        v_cos = c.dma("sync", "ld_s", cosb[:], cos_t.ap())
        v_sin = c.dma("sync", "ld_s", sinb[:], sin_t.ap())
        ins = None
        if c.on("vector"):
            nc_.vector.memset(negB[:], -EXP_SHIFT)
            ins = nc_.vector.memset(epsb[:], 1e-5)
        v_msets = c.inc(ins, "ve")

        if dbg_sdb0 is not None:
            c.wait("sync", "sp", 0)
            c.dma("sync", "sp", dbg_sdb0.ap(), sdb[:])

        c.barrier()

        v_xbt = 0
        for cc in range(32):
            v_xbt = c.dma("gpsimd", "g_xbt", xbT[:, cc * S:(cc + 1) * S],
                          xT.ap()[cc * 128:(cc + 1) * 128, :])

        state = {
            "wslab_reuse": [0, 0],
            "ps_guard": [0, 0, 0],
            "rt_war": 0,
        }

        def qk_tile(dt, is_k):
            wsrc = (wk_t if is_k else wq_t).ap()[dt]
            vw = [0, 0]
            for half in range(2):
                slot = half
                ssem = "g_wsl%d" % slot
                if state["wslab_reuse"][slot] > 0:
                    c.wait("gpsimd", "mm", state["wslab_reuse"][slot])
                src = wsrc[:, 16 * half:16 * (half + 1), :].rearrange("p a b -> p (a b)")
                vw[half] = c.dma("gpsimd", ssem, wslab[slot][:], src)
            c.wait("tensor", "g_wsl0", vw[0])
            c.wait("tensor", "g_wsl1", vw[1])
            c.wait("tensor", "g_xbt", v_xbt)
            last_mm = 0
            last_ve = 0
            for sb in range(4):
                bank = (dt * 4 + sb) % 3
                ps = qk_ps[bank]
                if state["rt_war"] > 0:
                    c.wait("vector", "ve", state["rt_war"])
                if state["ps_guard"][bank] > 0:
                    c.wait("tensor", "ve", state["ps_guard"][bank])
                ins = None
                if c.on("tensor"):
                    for cc in range(32):
                        ins = nc_.tensor.matmul(
                            ps[:], wslab[cc // 16][:, (cc % 16) * 128:(cc % 16 + 1) * 128],
                            xbT[:, cc * S + sb * 512: cc * S + (sb + 1) * 512],
                            start=(cc == 0), stop=(cc == 31))
                mm_v = c.inc(ins, "mm")
                last_mm = mm_v
                c.wait("vector", "mm", mm_v)
                c.wait("vector", "ld_s", max(v_cos, v_sin))
                cs = slice(sb * 512, (sb + 1) * 512)
                if is_k:
                    dst = KT[:, dt * S + sb * 512: dt * S + (sb + 1) * 512]
                else:
                    dst = qstage[dt % 2][:, cs]
                ins = None
                if c.on("vector"):
                    nc_.vector.tensor_mul(rt_t2[0:64, :], ps[64:128, :], sinb[0:64, cs])
                    nc_.vector.tensor_mul(rt_t2[64:128, :], ps[0:64, :], sinb[64:128, cs])
                    ins = nc_.vector.tensor_mul(rt_t1[:], ps[:], cosb[:, cs])
                ve_r = c.inc(ins, "ve")
                c.wait("vector", "ve", ve_r)
                ins = None
                if c.on("vector"):
                    ins = nc_.vector.tensor_add(dst, rt_t1[:], rt_t2[:])
                last_ve = c.inc(ins, "ve")
                state["rt_war"] = last_ve
                state["ps_guard"][bank] = last_ve
            state["wslab_reuse"][0] = last_mm
            state["wslab_reuse"][1] = last_mm
            return last_ve

        qspill = [0, 0]
        for dt in range(NH):
            if qspill[dt % 2] > 0:
                c.wait("vector", "sp", qspill[dt % 2])
            last_rope = qk_tile(dt, is_k=False)
            c.wait("sync", "ve", last_rope)
            qspill[dt % 2] = c.dma("sync", "sp", qt_s.ap()[dt], qstage[dt % 2][:])
        for kt in range(NKV):
            qk_tile(kt, is_k=True)

        c.barrier()

        vsp = [0, 0]
        c.wait("gpsimd", "sp", max(qspill))
        vslab_reuse = 0
        for dvq in range(4):
            if vslab_reuse > 0:
                c.wait("gpsimd", "mm", vslab_reuse)
            vw = c.dma("gpsimd", "g_vsl", vslab[:],
                       wv_t.ap()[dvq].rearrange("p a b -> p (a b)"))
            c.wait("tensor", "g_vsl", vw)
            for st in range(16):
                bank = st % 3
                ps = qk_ps[bank]
                if state["ps_guard"][bank] > 0:
                    c.wait("tensor", "ve", state["ps_guard"][bank])
                ins = None
                if c.on("tensor"):
                    for cc in range(32):
                        ins = nc_.tensor.matmul(
                            ps[:, 0:256],
                            xbT[:, cc * S + st * 128: cc * S + st * 128 + 128],
                            vslab[:, cc * 256:(cc + 1) * 256],
                            start=(cc == 0), stop=(cc == 31))
                mm_v = c.inc(ins, "mm")
                vslab_reuse = mm_v
                if vsp[st % 2] > 0:
                    c.wait("vector", "sp", vsp[st % 2])
                c.wait("vector", "mm", mm_v)
                ins = None
                if c.on("vector"):
                    ins = nc_.vector.tensor_copy(vstage[st % 2][:], ps[:, 0:256])
                ve_v = c.inc(ins, "ve")
                state["ps_guard"][bank] = ve_v
                c.wait("sync", "ve", ve_v)
                dst = v_s.ap()[2 * dvq:2 * dvq + 2, st].rearrange("g p d -> p g d")
                src3 = vstage[st % 2][:].rearrange("p (g d) -> p g d", g=2)
                vsp[st % 2] = c.dma("sync", "sp", dst, src3)

        if dbg_sdb1 is not None:
            c.dma("sync", "sp", dbg_sdb1.ap(), sdb[:])
        c.barrier()

        vg_guard = [0, 0]
        qh_guard = [0, 0]
        pt_guard = [0, 0, 0, 0]
        stb_ac = [0, 0]
        stb_ve = [0, 0]
        ot_guard = [0, 0]
        den_guard = [0, 0]
        bc_guard = 0
        woq_guard = [0] * 5
        cst_guard = [0, 0, 0, 0]
        last_vgl = 0
        x1sp = 0
        ybuf_ac_guard = 0

        recip_war = [0, 0]
        pending = None

        def emit_norm(pend):
            nonlocal bc_guard
            ph, par, recip_ve = pend
            c.wait("tensor", "ve", recip_ve)
            if bc_guard > 0:
                c.wait("tensor", "ve", bc_guard)
            ins = None
            if c.on("tensor"):
                ins = nc_.tensor.matmul(bc_ps[:], onrb[:], recip[par][:],
                                        start=True, stop=True)
            mm_bc = c.inc(ins, "mm")
            recip_war[par] = mm_bc
            c.wait("vector", "mm", mm_bc)
            if ot_guard[0] > 0 or ot_guard[1] > 0:
                c.wait("vector", "ve", max(ot_guard))
            ins = None
            if c.on("vector"):
                ins = nc_.vector.tensor_copy(recipb[:], bc_ps[:])
            bc_guard = c.inc(ins, "ve")
            c.wait("vector", "ve", bc_guard)
            ins = None
            if c.on("vector"):
                ins = nc_.vector.tensor_mul(
                    OTsl[:, ph * 512:(ph + 1) * 512], ot_ps[par][:], recipb[:])
            nv = c.inc(ins, "ve")
            ot_guard[par] = nv
            return nv

        for qb in range(4):
            last_exp = 0
            last_ve_head = 0
            nkt = 4 * qb + 4

            def load_head(h):
                nonlocal last_vgl
                g = h // 4
                if h % 4 == 0:
                    if vg_guard[g % 2] > 0:
                        c.wait("sync", "mm", vg_guard[g % 2])
                    last_vgl = c.dma("sync", "ld_s",
                                     vg[g % 2][:].rearrange("p (t d) -> p t d", t=16),
                                     v_s.ap()[g].rearrange("t p d -> p t d"))
                if qh_guard[h % 2] > 0:
                    c.wait("sync", "mm", qh_guard[h % 2])
                qhl = c.dma("sync", "ld_s", qh[h % 2][:],
                            qt_s.ap()[h][:, qb * 512:(qb + 1) * 512])
                return qhl, last_vgl

            qh_ld = {0: load_head(0)}
            for h in range(NH):
                g = h // 4

                def emit_score(kt):
                    nonlocal last_exp
                    j = kt - 4 * qb
                    vis0 = 128 * max(j, 0)
                    bank = kt % 2
                    spn = st_ps[bank]
                    if stb_ac[bank] > 0:
                        c.wait("tensor", "ac", stb_ac[bank])
                    if stb_ve[bank] > 0:
                        c.wait("tensor", "ve", stb_ve[bank])
                    ins = None
                    if c.on("tensor"):
                        ins = nc_.tensor.matmul(
                            spn[:, vis0:512],
                            KT[:, g * S + kt * 128: g * S + kt * 128 + 128],
                            qh[h % 2][:, vis0:512], start=True, stop=True)
                    mm_v = c.inc(ins, "mm")
                    ptn = pt[kt % 4]
                    if pt_guard[kt % 4] > 0:
                        c.wait("scalar", "mm", pt_guard[kt % 4])
                    c.wait("scalar", "mm", mm_v)
                    c.wait("scalar", "ve", v_msets)
                    ins = None
                    if c.on("scalar"):
                        ins = nc_.scalar.activation(
                            ptn[:, vis0:512], spn[:, vis0:512], AF.Exp,
                            bias=negB[:], scale=SCALE)
                    ac_v = c.inc(ins, "ac")
                    stb_ac[bank] = ac_v
                    last_exp = ac_v
                    mask_v = 0
                    if j >= 0:
                        c.wait("vector", "ac", ac_v)
                        c.wait("vector", "ld_s", v_tri)
                        ins = None
                        if c.on("vector"):
                            ins = nc_.vector.tensor_mul(
                                ptn[:, vis0:vis0 + 128],
                                ptn[:, vis0:vis0 + 128], trib[:])
                        mask_v = c.inc(ins, "ve")
                    return (ac_v, mask_v, vis0)

                qhl, vgl = qh_ld.pop(h)
                c.wait("tensor", "ld_s", max(qhl, vgl))
                pv_v = 0
                sc = {0: emit_score(0)}
                for kt in range(nkt):
                    if kt + 1 < nkt:
                        sc[kt + 1] = emit_score(kt + 1)
                    ac_v, mask_v, vis0 = sc.pop(kt)
                    if mask_v:
                        c.wait("tensor", "ve", mask_v)
                    else:
                        c.wait("tensor", "ac", ac_v)
                    c.wait("tensor", "ld_s", v_onc)
                    if kt == 0:
                        if den_guard[h % 2] > 0:
                            c.wait("tensor", "ve", den_guard[h % 2])
                        if ot_guard[h % 2] > 0:
                            c.wait("tensor", "ve", ot_guard[h % 2])
                    ins = None
                    if c.on("tensor"):
                        nc_.tensor.matmul(
                            den_ps[h % 2][:, vis0:512], oncb[:], pt[kt % 4][:, vis0:512],
                            start=(kt == 0), stop=(kt == nkt - 1))
                        ins = nc_.tensor.matmul(
                            ot_ps[h % 2][:, vis0:512],
                            vg[g % 2][:, kt * 128:kt * 128 + 128],
                            pt[kt % 4][:, vis0:512],
                            start=(kt == 0), stop=(kt == nkt - 1))
                    mm_v = c.inc(ins, "mm")
                    pt_guard[kt % 4] = mm_v
                    pv_v = mm_v
                    if kt == min(1, nkt - 1):
                        if pending is not None:
                            last_ve_head = emit_norm(pending)
                            pending = None
                        if h + 1 < NH:
                            qh_ld[h + 1] = load_head(h + 1)
                qh_guard[h % 2] = pv_v
                if h % 4 == 3:
                    vg_guard[g % 2] = pv_v
                c.wait("vector", "mm", pv_v)
                if recip_war[h % 2] > 0:
                    c.wait("vector", "mm", recip_war[h % 2])
                ins = None
                if c.on("vector"):
                    with nc_.allow_low_precision("1/den rounded to bf16; within the bf16 operand precision class"):
                        ins = nc_.vector.reciprocal(recip[h % 2][:], den_ps[h % 2][:])
                ve_v = c.inc(ins, "ve")
                den_guard[h % 2] = ve_v
                c.wait("vector", "ve", ve_v)
                pending = (h, h % 2, ve_v)
            if pending is not None:
                last_ve_head = emit_norm(pending)
                pending = None

            c.wait("tensor", "ac", last_exp)
            c.wait("tensor", "ve", last_ve_head)
            for cb in range(8):
                qvals = []
                for q in range(4):
                    slot = (cb * 4 + q) % 5
                    ssem = "g_wo%d" % slot
                    if woq_guard[slot] > 0:
                        c.wait("gpsimd", "mm", woq_guard[slot])
                    off = (cb * 4 + q) * 128 * 4096
                    vq = c.dma("gpsimd", ssem, woq[slot][:],
                               wo_t.ap()[off: off + 128 * 4096]
                               .rearrange("(p f) -> p f", p=128))
                    qvals.append((slot, vq))
                for slot, vq in qvals:
                    c.wait("tensor", "g_wo%d" % slot, vq)
                for st in range(4):
                    bank = st % 2
                    ps = st_ps[bank]
                    if stb_ve[bank] > 0:
                        c.wait("tensor", "ve", stb_ve[bank])
                    if cst_guard[st % 4] > 0:
                        c.wait("vector", "sp", cst_guard[st % 4])
                    ins = None
                    if c.on("tensor"):
                        for cc in range(32):
                            slot = (cb * 4 + cc // 8) % 5
                            ins = nc_.tensor.matmul(
                                ps[:],
                                OTsl[:, cc * 512 + st * 128: cc * 512 + st * 128 + 128],
                                woq[slot][:, (cc % 8) * 512:(cc % 8 + 1) * 512],
                                start=(cc == 0), stop=(cc == 31))
                    mm_v = c.inc(ins, "mm")
                    for q in range(4):
                        woq_guard[(cb * 4 + q) % 5] = mm_v
                    c.wait("vector", "mm", mm_v)
                    ins = None
                    if c.on("vector"):
                        ins = nc_.vector.tensor_copy(cstage[st % 4][:], ps[:])
                    ve_v = c.inc(ins, "ve")
                    stb_ve[bank] = ve_v
                    c.wait("sync", "ve", ve_v)
                    rows = qb * 512 + st * 128
                    cst_guard[st % 4] = c.dma(
                        "sync", "sp",
                        y_s.ap()[rows:rows + 128, cb * 512:(cb + 1) * 512],
                        cstage[st % 4][:])
            all_y = cst_guard[3]

            for st in range(4):
                rows = qb * 512 + st * 128
                c.wait("sync", "sp", max(all_y, x1sp))
                if ybuf_ac_guard > 0:
                    c.wait("sync", "ac", ybuf_ac_guard)
                vy = c.dma("sync", "ld_s", ybuf[:], y_s.ap()[rows:rows + 128, :])
                vx = c.dma("sync", "ld_s", xzbuf[:], x_nat.ap()[rows:rows + 128, :])
                c.wait("vector", "ld_s", vx)
                ins = None
                if c.on("vector"):
                    ins = nc_.vector.tensor_add(xzbuf[:], xzbuf[:], ybuf[:])
                ve_za = c.inc(ins, "ve")
                c.wait("vector", "ve", ve_za)
                ins = None
                if c.on("vector"):
                    ins = nc_.vector.reduce_sum(sm["s1"][:], xzbuf[:], AX.X)
                ve_z = c.inc(ins, "ve")
                c.wait("vector", "ve", ve_z)
                c.wait("scalar", "ve", ve_z)
                c.wait("scalar", "sp", x1sp)
                ins = None
                if c.on("scalar"):
                    ins = nc_.scalar.activation(ybuf[:], xzbuf[:], AF.Square,
                                                accum_out=sm["s2"][:])
                ac_sq = c.inc(ins, "ac")
                ybuf_ac_guard = ac_sq
                ins = None
                if c.on("vector"):
                    ins = nc_.vector.tensor_scalar_mul(sm["mean"][:], sm["s1"][:], 1.0 / H)
                ve_mean = c.inc(ins, "ve")
                c.wait("vector", "ve", ve_mean)
                c.wait("scalar", "ve", ve_mean)
                ins = None
                if c.on("scalar"):
                    ins = nc_.scalar.activation(sm["m2"][:], sm["mean"][:], AF.Square)
                ac_m2 = c.inc(ins, "ac")
                c.wait("scalar", "ac", ac_m2)
                c.wait("vector", "ac", max(ac_sq, ac_m2))
                ins = None
                if c.on("vector"):
                    ins = nc_.vector.scalar_tensor_tensor(
                        sm["var"][:], sm["s2"][:], 1.0 / H, sm["m2"][:],
                        ALU.mult, ALU.subtract)
                ve_var = c.inc(ins, "ve")
                c.wait("vector", "ve", ve_var)
                c.wait("scalar", "ve", ve_var)
                ins = None
                if c.on("scalar"):
                    nc_.scalar.activation(sm["lnv"][:], sm["var"][:], AF.Ln, bias=epsb[:])
                    ins = nc_.scalar.activation(sm["rstd"][:], sm["lnv"][:], AF.Exp,
                                                scale=-0.5)
                ac_rstd = c.inc(ins, "ac")
                c.wait("scalar", "ac", ac_rstd)
                c.wait("vector", "ac", ac_rstd)
                ins = None
                if c.on("vector"):
                    ins = nc_.vector.scalar_tensor_tensor(
                        sm["nmr"][:], sm["mean"][:], -1.0, sm["rstd"][:],
                        ALU.mult, ALU.mult)
                ve_nmr = c.inc(ins, "ve")
                c.wait("vector", "ve", ve_nmr)
                c.wait("scalar", "ve", ve_nmr)
                ins = None
                if c.on("scalar"):
                    ins = nc_.scalar.activation(x1buf[:], xzbuf[:], AF.Identity,
                                                scale=sm["rstd"][:], bias=sm["nmr"][:])
                ac_x1 = c.inc(ins, "ac")
                c.wait("sync", "ac", ac_x1)
                x1sp = c.dma("sync", "sp", x1_s.ap()[rows:rows + 128, :], x1buf[:])

        c.barrier()

        x1t_guard = 0
        wg_guard = [0, 0]
        wd_guard = [0, 0]
        h_guard = 0
        gate_guard = 0
        up_guard = 0
        dn_guard = [0, 0]
        out_guard = 0
        out_sp = 0
        x1re_guard = 0
        x1re_ac_guard = 0
        ic_groups = [(0, 11), (11, 22), (22, 33), (33, 44),
                     (44, 55), (55, 66), (66, 77), (77, 86)]
        for ch in range(NCH):
            if os.environ.get("KSER_BAR"):
                c.barrier()
            if x1t_guard > 0:
                c.wait("sync", "mm", x1t_guard)
            vxt = 0
            for cc in range(32):
                vxt = c.dma("sync", "ld_s",
                            x1T[:, cc * SB:(cc + 1) * SB],
                            x1_s.ap()[ch * SB:(ch + 1) * SB, cc * 128:(cc + 1) * 128],
                            transpose=True)
            mm_u = 0
            ve_h = 0
            for it in range(NIT):
                slot = it % 2
                ssem = "g_wg%d" % slot
                if wg_guard[slot] > 0:
                    c.wait("gpsimd", "mm", wg_guard[slot])
                c.dma("gpsimd", ssem, wgr[slot][:],
                      wg_t.ap()[it].rearrange("p a b -> p (a b)"))
                vu_w = c.dma("gpsimd", ssem, wur[slot][:],
                             wu_t.ap()[it].rearrange("p a b -> p (a b)"))
                c.wait("tensor", ssem, vu_w)
                c.wait("tensor", "ld_s", vxt)
                if it == 0 and h_guard > 0:
                    c.wait("tensor", "mm", h_guard)
                if gate_guard > 0:
                    c.wait("tensor", "ac", gate_guard)
                ins = None
                if c.on("tensor"):
                    for cc in range(32):
                        ins = nc_.tensor.matmul(
                            gate_ps[:], wgr[slot][:, cc * 128:(cc + 1) * 128],
                            x1T[:, cc * SB:(cc + 1) * SB],
                            start=(cc == 0), stop=(cc == 31))
                mm_g = c.inc(ins, "mm")
                if up_guard > 0:
                    c.wait("tensor", "ve", up_guard)
                ins = None
                if c.on("tensor"):
                    for cc in range(32):
                        ins = nc_.tensor.matmul(
                            up_ps[:], wur[slot][:, cc * 128:(cc + 1) * 128],
                            x1T[:, cc * SB:(cc + 1) * SB],
                            start=(cc == 0), stop=(cc == 31))
                mm_u = c.inc(ins, "mm")
                wg_guard[slot] = mm_u
                c.wait("scalar", "mm", mm_g)
                c.wait("scalar", "ld_s", v_sg)
                if up_guard > 0:
                    c.wait("scalar", "ve", up_guard)
                ins = None
                if c.on("scalar"):
                    ins = nc_.scalar.activation(t1buf[:], gate_ps[:], AF.Silu,
                                                scale=sgb[:, it:it + 1])
                ac_t1 = c.inc(ins, "ac")
                gate_guard = ac_t1
                c.wait("vector", "mm", mm_u)
                c.wait("vector", "ac", ac_t1)
                c.wait("vector", "ld_s", v_su)
                ins = None
                if c.on("vector"):
                    ins = nc_.vector.scalar_tensor_tensor(
                        hstore[:, it * SB:(it + 1) * SB], up_ps[:],
                        sub[:, it:it + 1], t1buf[:], ALU.mult, ALU.mult)
                ve_h = c.inc(ins, "ve")
                up_guard = ve_h
            x1t_guard = mm_u
            if os.environ.get("KSER_BAR"):
                c.barrier()
            if dbg_h2 is not None and ch == 0:
                c.wait("sync", "ve", ve_h)
                c.dma("sync", "sp", dbg_h2.ap(), hstore[:])
                c.dma("sync", "sp", dbg_sdb.ap(), sdb[:])
            c.wait("tensor", "ve", ve_h)
            mm_dn = 0
            for ob in range(8):
                for gi, (ic0, ic1) in enumerate(ic_groups):
                    slot = (ob * 8 + gi) % 2
                    ssem = "g_wd%d" % slot
                    if wd_guard[slot] > 0:
                        c.wait("gpsimd", "mm", wd_guard[slot])
                    if os.environ.get("KSER_WD"):
                        c.wait("gpsimd", "mm", c.cnt["mm"])
                    ni = ic1 - ic0
                    off = (ob * NIT + ic0) * 128 * 512
                    src = wd_t.ap()[off: off + 128 * ni * 512].rearrange(
                        "(p f) -> p f", p=128)
                    vd_w = c.dma("gpsimd", ssem,
                                 wdring[slot][:, 0:ni * 512], src)
                    c.wait("tensor", ssem, vd_w)
                    if gi == 0:
                        for st in range(2):
                            if dn_guard[st] > 0:
                                c.wait("tensor", "ve", dn_guard[st])
                    ins = None
                    if c.on("tensor"):
                        for ic in range(ic0, ic1):
                            for st in range(2):
                                ins = nc_.tensor.matmul(
                                    dn_ps[st][:],
                                    hstore[:, ic * SB + st * 128: ic * SB + st * 128 + 128],
                                    wdring[slot][:, (ic - ic0) * 512:(ic - ic0 + 1) * 512],
                                    start=(ic == 0), stop=(ic == NIT - 1))
                    mm_dn = c.inc(ins, "mm")
                    if os.environ.get("KSER_DN"):
                        c.wait("tensor", "mm", mm_dn)
                    wd_guard[slot] = mm_dn
                    if dbg_wd is not None and ch == 0 and ob == 0 and gi < 2:
                        c.wait("sync", "mm", mm_dn)
                        wddump = c.dma("sync", "sp", dbg_wd.ap()[gi], wdring[slot][:])
                        c.wait("gpsimd", "sp", wddump)
                for st in range(2):
                    c.wait("vector", "mm", mm_dn)
                    if dbg_dn is not None and ch == 0 and ob == 0:
                        ins = None
                        if c.on("vector"):
                            ins = nc_.vector.tensor_copy(zbuf[:, st * 512:(st + 1) * 512],
                                                         dn_ps[st][:])
                        dvv = c.inc(ins, "ve")
                        c.wait("sync", "ve", dvv)
                        ddump = c.dma("sync", "sp", dbg_dn.ap()[st],
                                      zbuf[:, st * 512:(st + 1) * 512])
                        c.wait("vector", "sp", ddump)
                    ins = None
                    if c.on("vector"):
                        ins = nc_.vector.tensor_mul(
                            yst[:, st * H + ob * 512: st * H + (ob + 1) * 512],
                            dn_ps[st][:], sdb[:, ob * 512:(ob + 1) * 512])
                    ve_v = c.inc(ins, "ve")
                    dn_guard[st] = ve_v
            y2_done = max(dn_guard)
            h_guard = mm_dn
            if os.environ.get("KSER_BAR"):
                c.barrier()
            if dbg_x1t is not None and ch == 0:
                c.wait("sync", "ve", y2_done)
                c.dma("sync", "sp", dbg_x1t.ap(), x1T[:])
                c.dma("sync", "sp", dbg_h.ap(), hstore[:])
                c.dma("sync", "sp", dbg_y2.ap(), yst[:])

            for st in range(2):
                rows = ch * SB + st * 128
                if x1re_guard > 0:
                    c.wait("sync", "ve", x1re_guard)
                if x1re_ac_guard > 0:
                    c.wait("sync", "ac", x1re_ac_guard)
                vx1 = c.dma("sync", "ld_s", x1re[:], x1_s.ap()[rows:rows + 128, :])
                c.wait("vector", "ld_s", vx1)
                c.wait("vector", "ve", y2_done)
                if out_guard > 0:
                    c.wait("vector", "ac", out_guard)
                ins = None
                if c.on("vector"):
                    ins = nc_.vector.tensor_add(zbuf[:], x1re[:], yst[:, st * H:(st + 1) * H])
                ve_za = c.inc(ins, "ve")
                c.wait("vector", "ve", ve_za)
                ins = None
                if c.on("vector"):
                    ins = nc_.vector.reduce_sum(sm2["s1"][:], zbuf[:], AX.X)
                ve_z = c.inc(ins, "ve")
                x1re_guard = ve_z
                c.wait("vector", "ve", ve_z)
                c.wait("scalar", "ve", ve_z)
                ins = None
                if c.on("scalar"):
                    ins = nc_.scalar.activation(x1re[:], zbuf[:], AF.Square,
                                                accum_out=sm2["s2"][:])
                ac_sq = c.inc(ins, "ac")
                x1re_ac_guard = ac_sq
                ins = None
                if c.on("vector"):
                    ins = nc_.vector.tensor_scalar_mul(sm2["mean"][:], sm2["s1"][:], 1.0 / H)
                ve_mean = c.inc(ins, "ve")
                c.wait("vector", "ve", ve_mean)
                c.wait("scalar", "ve", ve_mean)
                ins = None
                if c.on("scalar"):
                    ins = nc_.scalar.activation(sm2["m2"][:], sm2["mean"][:], AF.Square)
                ac_m2 = c.inc(ins, "ac")
                c.wait("scalar", "ac", ac_m2)
                c.wait("vector", "ac", max(ac_sq, ac_m2))
                ins = None
                if c.on("vector"):
                    ins = nc_.vector.scalar_tensor_tensor(
                        sm2["var"][:], sm2["s2"][:], 1.0 / H, sm2["m2"][:],
                        ALU.mult, ALU.subtract)
                ve_var = c.inc(ins, "ve")
                c.wait("vector", "ve", ve_var)
                c.wait("scalar", "ve", ve_var)
                ins = None
                if c.on("scalar"):
                    nc_.scalar.activation(sm2["lnv"][:], sm2["var"][:], AF.Ln, bias=epsb[:])
                    ins = nc_.scalar.activation(sm2["rstd"][:], sm2["lnv"][:], AF.Exp,
                                                scale=-0.5)
                ac_rstd = c.inc(ins, "ac")
                c.wait("scalar", "ac", ac_rstd)
                c.wait("vector", "ac", ac_rstd)
                ins = None
                if c.on("vector"):
                    ins = nc_.vector.scalar_tensor_tensor(
                        sm2["nmr"][:], sm2["mean"][:], -1.0, sm2["rstd"][:],
                        ALU.mult, ALU.mult)
                ve_nmr = c.inc(ins, "ve")
                c.wait("vector", "ve", ve_nmr)
                c.wait("scalar", "ve", ve_nmr)
                for hf in range(4):
                    if out_guard > 0:
                        c.wait("scalar", "so", out_sp)
                    ins = None
                    if c.on("scalar"):
                        ins = nc_.scalar.activation(
                            outst[:], zbuf[:, hf * 1024:(hf + 1) * 1024], AF.Identity,
                            scale=sm2["rstd"][:], bias=sm2["nmr"][:])
                    ac_o = c.inc(ins, "ac")
                    out_guard = ac_o
                    c.wait("sync", "ac", ac_o)
                    out_sp = c.dma("sync", "so",
                                   out.ap()[rows:rows + 128, hf * 1024:(hf + 1) * 1024],
                                   outst[:])

        c.wait("sync", "so", c.cnt["so"])
        c.barrier()

    return nc, sems, emit


def _round_bf16(a):
    return np.ascontiguousarray(a).astype(BFNP)


def _host_prep(inputs):
    X = np.asarray(inputs["L0_attn_hidden_states"], dtype=np.float32)
    Wq = np.asarray(inputs["L0_attn_Wq"], dtype=np.float32)
    Wk = np.asarray(inputs["L0_attn_Wk"], dtype=np.float32)
    Wv = np.asarray(inputs["L0_attn_Wv"], dtype=np.float32)
    Wo = np.asarray(inputs["L0_attn_Wo"], dtype=np.float32)
    wg = np.asarray(inputs["L0_ffn_w_gate"])
    sg = np.asarray(inputs["L0_ffn_s_gate"], dtype=np.float32)
    wu = np.asarray(inputs["L0_ffn_w_up"])
    su = np.asarray(inputs["L0_ffn_s_up"], dtype=np.float32)
    wd = np.asarray(inputs["L0_ffn_w_down"])
    sd = np.asarray(inputs["L0_ffn_s_down"], dtype=np.float32)

    sh = {}
    sh["wq_t"] = np.ascontiguousarray(Wq.T.reshape(32, 128, 32, 128).transpose(2, 1, 0, 3))
    sh["wk_t"] = np.ascontiguousarray(Wk.T.reshape(32, 128, 8, 128).transpose(2, 1, 0, 3))
    sh["wv_t"] = np.ascontiguousarray(Wv.T.reshape(32, 128, 4, 256).transpose(2, 1, 0, 3))
    WoT_r = Wo.T.reshape(4, 8, 128, 8, 512)
    sh["wo_t"] = np.ascontiguousarray(WoT_r.transpose(3, 0, 2, 1, 4)).reshape(-1)
    sh["wg_t"] = np.ascontiguousarray(wg.T.reshape(32, 128, NIT, 128).transpose(2, 1, 0, 3))
    sh["wu_t"] = np.ascontiguousarray(wu.T.reshape(32, 128, NIT, 128).transpose(2, 1, 0, 3))
    WdT_r = wd.T.reshape(NIT, 128, 8, 512)
    ic_groups = [(0, 11), (11, 22), (22, 33), (33, 44),
                 (44, 55), (55, 66), (66, 77), (77, 86)]
    blocks = []
    for ob in range(8):
        for (ic0, ic1) in ic_groups:
            blocks.append(np.ascontiguousarray(
                WdT_r[ic0:ic1, :, ob, :].transpose(1, 0, 2)).reshape(-1))
    sh["wd_t"] = np.concatenate(blocks)
    sh["sg_t"] = np.ascontiguousarray(sg.reshape(NIT, 128).T)
    sh["su_t"] = np.ascontiguousarray(su.reshape(NIT, 128).T)
    sh["sdb_in"] = np.ascontiguousarray(np.broadcast_to(sd.reshape(1, H), (128, H)))

    inv = 1.0 / 10000.0 ** (np.arange(0, D, 2, dtype=np.float32) / D)
    t = np.arange(S, dtype=np.float32)
    fr = np.outer(t, inv)
    emb = np.concatenate([fr, fr], axis=-1)
    sh["cos_t"] = _round_bf16(np.cos(emb).astype(np.float32).T)
    sin2T = np.sin(emb).astype(np.float32).T.copy()
    sin2T[0:64, :] *= -1.0
    sh["sin_t"] = _round_bf16(sin2T)
    sh["tri_t"] = _round_bf16(np.triu(np.ones((128, 128), dtype=np.float32)))
    sh["onc_t"] = _round_bf16(np.ones((128, 1), dtype=np.float32))
    sh["onr_t"] = np.ones((1, 128), dtype=BFNP)

    in_maps = []
    for b in range(8):
        m = dict(sh)
        m["x_nat"] = np.ascontiguousarray(X[b])
        m["xT"] = np.ascontiguousarray(X[b].T)
        in_maps.append(m)
    return in_maps


def kernel(**inputs):
    in_maps = _host_prep(inputs)
    nc, sems, emit = build_program()
    with nc.Block() as block:
        for name in ENGINES:
            getattr(block, name)(lambda e, n=name: emit(Ctx(nc, n, sems)))
    res = run_bass_kernel_spmd(nc, in_maps, core_ids=list(range(8)))
    outs = [r["out"] for r in res.results]
    return np.stack(outs, axis=0).astype(np.float32)


# version 06
# speedup vs baseline: 1.4321x; 1.0798x over previous
import os
import numpy as np
import ml_dtypes
import concourse.bass as bass
import concourse.mybir as mybir
from concourse.bass_utils import run_bass_kernel_spmd

F32 = mybir.dt.float32
BF16 = mybir.dt.bfloat16
I8 = mybir.dt.int8
AF = mybir.ActivationFunctionType
ALU = mybir.AluOpType
AX = mybir.AxisListType
BFNP = ml_dtypes.bfloat16

S = 2048
H = 4096
D = 128
NH = 32
NKV = 8
INTER = 11008
NIT = INTER // 128
SB = 256
NCH = S // SB
EXP_SHIFT = 25.0
SCALE = 1.0 / float(np.sqrt(np.float32(D)))

ENGINES = ("tensor", "vector", "scalar", "sync", "gpsimd")

SEM_NAMES = [
    "mm", "ve", "ac",
    "ld_s", "sp", "so",
    "g_xbt",
    "g_wsl0", "g_wsl1", "g_wsl2", "g_wsl3",
    "g_vsl",
    "g_wo0", "g_wo1", "g_wo2", "g_wo3", "g_wo4",
    "g_wg0", "g_wg1",
    "g_wd0", "g_wd1",
    "bar",
]


class Ctx:
    def __init__(self, nc, eng_name, sems):
        self.nc = nc
        self.eng = eng_name
        self.e = {"tensor": nc.tensor, "vector": nc.vector, "scalar": nc.scalar,
                  "sync": nc.sync, "gpsimd": nc.gpsimd}[eng_name]
        self.sems = sems
        self.cnt = {n: 0 for n in sems}
        self.nbar = 0

    def on(self, name):
        return self.eng == name

    def inc(self, ins, sem, n=1):
        self.cnt[sem] += n
        if ins is not None:
            ins.then_inc(self.sems[sem], n)
        return self.cnt[sem]

    def wait(self, owner, sem, val):
        if self.eng == owner and val > 0:
            self.e.wait_ge(self.sems[sem], val)

    def dma(self, owner, sem, dst, src, transpose=False):
        ins = None
        if self.on(owner):
            eng = {"sync": self.nc.sync, "gpsimd": self.nc.gpsimd}[owner]
            if transpose:
                ins = eng.dma_start(out=dst, in_=src, transpose=True)
            else:
                ins = eng.dma_start(out=dst, in_=src)
        return self.inc(ins, sem, 16)

    def barrier(self):
        self.nbar += 1
        self.cnt["bar"] += 5
        self.e.sem_inc(self.sems["bar"], 1)
        self.e.wait_ge(self.sems["bar"], 5 * self.nbar)


def build_program(debug_scratch=False):
    nc = bass.Bass("TRN2", target_bir_lowering=False, debug=False)
    SK = "ExternalOutput" if debug_scratch else "Internal"

    x_nat = nc.dram_tensor("x_nat", [S, H], F32, kind="ExternalInput")
    xT = nc.dram_tensor("xT", [H, S], F32, kind="ExternalInput")
    wq_t = nc.dram_tensor("wq_t", [32, 128, 32, 128], F32, kind="ExternalInput")
    wk_t = nc.dram_tensor("wk_t", [8, 128, 32, 128], F32, kind="ExternalInput")
    wv_t = nc.dram_tensor("wv_t", [4, 128, 32, 256], F32, kind="ExternalInput")
    wo_t = nc.dram_tensor("wo_t", [8 * 4 * 128 * 4096], F32, kind="ExternalInput")
    wg_t = nc.dram_tensor("wg_t", [NIT, 128, 32, 128], I8, kind="ExternalInput")
    wu_t = nc.dram_tensor("wu_t", [NIT, 128, 32, 128], I8, kind="ExternalInput")
    wd_t = nc.dram_tensor("wd_t", [8 * NIT * 128 * 512], I8, kind="ExternalInput")
    sg_t = nc.dram_tensor("sg_t", [128, NIT], F32, kind="ExternalInput")
    su_t = nc.dram_tensor("su_t", [128, NIT], F32, kind="ExternalInput")
    sdb_in = nc.dram_tensor("sdb_in", [128, H], F32, kind="ExternalInput")
    cos_t = nc.dram_tensor("cos_t", [128, S], BF16, kind="ExternalInput")
    sin_t = nc.dram_tensor("sin_t", [128, S], BF16, kind="ExternalInput")
    tri_t = nc.dram_tensor("tri_t", [128, 128], BF16, kind="ExternalInput")
    onc_t = nc.dram_tensor("onc_t", [128, 1], BF16, kind="ExternalInput")
    onr_t = nc.dram_tensor("onr_t", [1, 128], BF16, kind="ExternalInput")

    out = nc.dram_tensor("out", [S, H], F32, kind="ExternalOutput")

    qt_s = nc.dram_tensor("qt_s", [NH, 128, S], BF16, kind=SK)
    v_s = nc.dram_tensor("v_s", [NKV, 16, 128, 128], BF16, kind=SK)
    y_s = nc.dram_tensor("y_s", [S, H], BF16, kind=SK)
    x1_s = nc.dram_tensor("x1_s", [S, H], BF16, kind=SK)
    if debug_scratch:
        dbg_x1t = nc.dram_tensor("dbg_x1t", [128, 32 * SB], BF16, kind="ExternalOutput")
        dbg_h = nc.dram_tensor("dbg_h", [128, NIT * SB], BF16, kind="ExternalOutput")
        dbg_y2 = nc.dram_tensor("dbg_y2", [128, 2 * H], BF16, kind="ExternalOutput")
        dbg_wd = nc.dram_tensor("dbg_wd", [2, 128, 11 * 512], BF16, kind="ExternalOutput")
        dbg_h2 = nc.dram_tensor("dbg_h2", [128, NIT * SB], BF16, kind="ExternalOutput")
        dbg_sdb = nc.dram_tensor("dbg_sdb", [128, H], F32, kind="ExternalOutput")
        dbg_sdb0 = nc.dram_tensor("dbg_sdb0", [128, H], F32, kind="ExternalOutput")
        dbg_sdb1 = nc.dram_tensor("dbg_sdb1", [128, H], F32, kind="ExternalOutput")
        dbg_dn = nc.dram_tensor("dbg_dn", [2, 128, 512], F32, kind="ExternalOutput")
    else:
        dbg_x1t = dbg_h = dbg_y2 = dbg_wd = dbg_h2 = dbg_sdb = dbg_dn = None
        dbg_sdb0 = dbg_sdb1 = None

    KT = nc.alloc_sbuf_tensor("KT", [128, NKV * S], BF16)
    sgb = nc.alloc_sbuf_tensor("sgb", [128, NIT], F32)
    sub = nc.alloc_sbuf_tensor("sub", [128, NIT], F32)
    sdb = nc.alloc_sbuf_tensor("sdb", [128, H], F32)
    trib = nc.alloc_sbuf_tensor("trib", [128, 128], BF16)
    oncb = nc.alloc_sbuf_tensor("oncb", [128, 1], BF16)
    onrb = nc.alloc_sbuf_tensor("onrb", [1, 128], BF16)
    negB = nc.alloc_sbuf_tensor("negB", [128, 1], F32)
    epsb = nc.alloc_sbuf_tensor("epsb", [128, 1], F32)

    mark = nc.sbuf_base

    xbT = nc.alloc_sbuf_tensor("xbT", [128, 32 * S], BF16)
    a_sub = nc.sbuf_base
    cosb = nc.alloc_sbuf_tensor("cosb", [128, S], BF16)
    sinb = nc.alloc_sbuf_tensor("sinb", [128, S], BF16)
    wslab = [nc.alloc_sbuf_tensor(f"wslab{i}", [128, 2048], BF16) for i in range(4)]
    qstage = [nc.alloc_sbuf_tensor(f"qstage{i}", [128, S // 2], BF16) for i in range(2)]
    rt_t1 = nc.alloc_sbuf_tensor("rt_t1", [128, 512], BF16)
    rt_t2 = nc.alloc_sbuf_tensor("rt_t2", [128, 512], BF16)
    a_peak = nc.sbuf_base
    nc.sbuf_base = a_sub
    vslab = nc.alloc_sbuf_tensor("vslab", [128, 32 * 256], BF16)
    vstage = [nc.alloc_sbuf_tensor(f"vstage{i}", [128, 256], BF16) for i in range(2)]
    a_peak = max(a_peak, nc.sbuf_base)
    nc.sbuf_base = mark

    OTsl = nc.alloc_sbuf_tensor("OTsl", [128, NH * 512], BF16)
    qh = [nc.alloc_sbuf_tensor(f"qh{i}", [128, 512], BF16) for i in range(2)]
    vg = [nc.alloc_sbuf_tensor(f"vg{i}", [128, 16 * 128], BF16) for i in range(2)]
    pt = [nc.alloc_sbuf_tensor(f"pt{i}", [128, 512], BF16) for i in range(4)]
    woq = [nc.alloc_sbuf_tensor(f"woq{i}", [128, 8 * 512], BF16) for i in range(5)]
    cstage = [nc.alloc_sbuf_tensor(f"cstage{i}", [128, 512], BF16) for i in range(4)]
    recip = [nc.alloc_sbuf_tensor(f"recip{i}", [1, 512], BF16) for i in range(2)]
    recipb = nc.alloc_sbuf_tensor("recipb", [128, 512], F32)
    ybuf = nc.alloc_sbuf_tensor("ybuf", [128, H], BF16)
    xzbuf = nc.alloc_sbuf_tensor("xzbuf", [128, H], F32)
    x1buf = nc.alloc_sbuf_tensor("x1buf", [128, H], BF16)
    sm = {n: nc.alloc_sbuf_tensor(f"sm_{n}", [128, 1], F32)
          for n in ("s1", "s2", "mean", "m2", "var", "lnv", "rstd", "nmr")}
    bc_peak = nc.sbuf_base
    nc.sbuf_base = mark

    hstore = nc.alloc_sbuf_tensor("hstore", [128, NIT * SB], BF16)
    wdring = [nc.alloc_sbuf_tensor(f"wdring{i}", [128, 11 * 512], BF16) for i in range(2)]
    wgr = [nc.alloc_sbuf_tensor(f"wgr{i}", [128, 4096], BF16) for i in range(2)]
    wur = [nc.alloc_sbuf_tensor(f"wur{i}", [128, 4096], BF16) for i in range(2)]
    x1T = nc.alloc_sbuf_tensor("x1T", [128, 32 * SB], BF16, is_transpose_dest=True)
    t1buf = nc.alloc_sbuf_tensor("t1buf", [128, SB], BF16)
    yst = nc.alloc_sbuf_tensor("yst", [128, 2 * H], BF16)
    x1re = nc.alloc_sbuf_tensor("x1re", [128, H], BF16)
    zbuf = nc.alloc_sbuf_tensor("zbuf", [128, H], F32)
    outst = nc.alloc_sbuf_tensor("outst", [128, H // 4], F32)
    sm2 = {n: nc.alloc_sbuf_tensor(f"sm2_{n}", [128, 1], F32)
           for n in ("s1", "s2", "mean", "m2", "var", "lnv", "rstd", "nmr")}
    d_peak = nc.sbuf_base
    nc.sbuf_base = mark

    assert max(a_peak, bc_peak, d_peak) <= nc.sbuf_top, \
        (a_peak, bc_peak, d_peak, nc.sbuf_top)

    pmark = nc.psum_base
    qk_ps = [nc.alloc_psum_tensor(f"qk_ps{i}", [128, 512], F32) for i in range(3)]
    nc.psum_base = pmark
    st_ps = [nc.alloc_psum_tensor(f"st_ps{i}", [128, 512], F32) for i in range(3)]
    ot_ps = [nc.alloc_psum_tensor(f"ot_ps{i}", [128, 512], F32) for i in range(2)]
    den_ps = [nc.alloc_psum_tensor(f"den_ps{i}", [1, 512], F32) for i in range(2)]
    bc_ps = nc.alloc_psum_tensor("bc_ps", [128, 512], F32)
    nc.psum_base = pmark
    gate_ps = nc.alloc_psum_tensor("gate_ps", [128, SB], F32)
    up_ps = nc.alloc_psum_tensor("up_ps", [128, SB], F32)
    dn_ps = [nc.alloc_psum_tensor(f"dn_ps{i}", [128, 512], F32) for i in range(2)]
    nc.psum_base = pmark

    sems = {n: nc.alloc_semaphore(name=f"s_{n}") for n in SEM_NAMES}

    def emit(c: Ctx):
        nc_ = c.nc

        v_tri = c.dma("sync", "ld_s", trib[:], tri_t.ap())
        v_onc = c.dma("sync", "ld_s", oncb[:], onc_t.ap())
        v_onr = c.dma("sync", "ld_s", onrb[:], onr_t.ap())
        v_sg = c.dma("sync", "ld_s", sgb[:], sg_t.ap())
        v_su = c.dma("sync", "ld_s", sub[:], su_t.ap())
        v_sdr = c.dma("sync", "ld_s", sdb[:], sdb_in.ap())
        v_cos = c.dma("sync", "ld_s", cosb[:], cos_t.ap())
        v_sin = c.dma("sync", "ld_s", sinb[:], sin_t.ap())
        ins = None
        if c.on("vector"):
            nc_.vector.memset(negB[:], -EXP_SHIFT)
            ins = nc_.vector.memset(epsb[:], 1e-5)
        v_msets = c.inc(ins, "ve")

        if dbg_sdb0 is not None:
            c.wait("sync", "sp", 0)
            c.dma("sync", "sp", dbg_sdb0.ap(), sdb[:])

        c.barrier()

        v_xbt = 0
        for cc in range(32):
            v_xbt = c.dma("gpsimd", "g_xbt", xbT[:, cc * S:(cc + 1) * S],
                          xT.ap()[cc * 128:(cc + 1) * 128, :])

        state = {
            "wslab_reuse": [0, 0, 0, 0],
            "ps_guard": [0, 0, 0],
            "rt_war": 0,
        }

        def qk_tile(dt, seq, is_k):
            wsrc = (wk_t if is_k else wq_t).ap()[dt]
            slots = [(2 * seq) % 4, (2 * seq + 1) % 4]
            vw = []
            for half in range(2):
                slot = slots[half]
                ssem = "g_wsl%d" % slot
                if state["wslab_reuse"][slot] > 0:
                    c.wait("gpsimd", "mm", state["wslab_reuse"][slot])
                src = wsrc[:, 16 * half:16 * (half + 1), :].rearrange("p a b -> p (a b)")
                vw.append(c.dma("gpsimd", "g_wsl%d" % slot, wslab[slot][:], src))
            for half in range(2):
                c.wait("tensor", "g_wsl%d" % slots[half], vw[half])
            c.wait("tensor", "g_xbt", v_xbt)
            last_mm = 0
            last_ve = 0
            for sb in range(4):
                bank = (dt * 4 + sb) % 3
                ps = qk_ps[bank]
                if state["rt_war"] > 0:
                    c.wait("vector", "ve", state["rt_war"])
                if state["ps_guard"][bank] > 0:
                    c.wait("tensor", "ve", state["ps_guard"][bank])
                ins = None
                if c.on("tensor"):
                    for cc in range(32):
                        ins = nc_.tensor.matmul(
                            ps[:], wslab[slots[cc // 16]][:, (cc % 16) * 128:(cc % 16 + 1) * 128],
                            xbT[:, cc * S + sb * 512: cc * S + (sb + 1) * 512],
                            start=(cc == 0), stop=(cc == 31))
                mm_v = c.inc(ins, "mm")
                last_mm = mm_v
                c.wait("vector", "mm", mm_v)
                c.wait("vector", "ld_s", max(v_cos, v_sin))
                cs = slice(sb * 512, (sb + 1) * 512)
                if is_k:
                    dst = KT[:, dt * S + sb * 512: dt * S + (sb + 1) * 512]
                else:
                    dst = qstage[sb // 2][:, (sb % 2) * 512:(sb % 2 + 1) * 512]
                ins = None
                if c.on("vector"):
                    nc_.vector.tensor_mul(rt_t2[0:64, :], ps[64:128, :], sinb[0:64, cs])
                    nc_.vector.tensor_mul(rt_t2[64:128, :], ps[0:64, :], sinb[64:128, cs])
                    ins = nc_.vector.tensor_mul(rt_t1[:], ps[:], cosb[:, cs])
                ve_r = c.inc(ins, "ve")
                c.wait("vector", "ve", ve_r)
                ins = None
                if c.on("vector"):
                    ins = nc_.vector.tensor_add(dst, rt_t1[:], rt_t2[:])
                last_ve = c.inc(ins, "ve")
                state["rt_war"] = last_ve
                state["ps_guard"][bank] = last_ve
            state["wslab_reuse"][slots[0]] = last_mm
            state["wslab_reuse"][slots[1]] = last_mm
            return last_ve

        qspill = [0, 0]
        for dt in range(NH):
            c.wait("vector", "sp", max(qspill))
            last_rope = qk_tile(dt, dt, is_k=False)
            c.wait("sync", "ve", last_rope)
            for hf in range(2):
                qspill[hf] = c.dma("sync", "sp",
                                   qt_s.ap()[dt][:, hf * 1024:(hf + 1) * 1024],
                                   qstage[hf][:])
        for kt in range(NKV):
            qk_tile(kt, NH + kt, is_k=True)

        c.barrier()

        vsp = [0, 0]
        c.wait("gpsimd", "sp", max(qspill))
        vslab_reuse = 0
        for dvq in range(4):
            if vslab_reuse > 0:
                c.wait("gpsimd", "mm", vslab_reuse)
            vw = c.dma("gpsimd", "g_vsl", vslab[:],
                       wv_t.ap()[dvq].rearrange("p a b -> p (a b)"))
            c.wait("tensor", "g_vsl", vw)
            for st in range(16):
                bank = st % 3
                ps = qk_ps[bank]
                if state["ps_guard"][bank] > 0:
                    c.wait("tensor", "ve", state["ps_guard"][bank])
                ins = None
                if c.on("tensor"):
                    for cc in range(32):
                        ins = nc_.tensor.matmul(
                            ps[:, 0:256],
                            xbT[:, cc * S + st * 128: cc * S + st * 128 + 128],
                            vslab[:, cc * 256:(cc + 1) * 256],
                            start=(cc == 0), stop=(cc == 31))
                mm_v = c.inc(ins, "mm")
                vslab_reuse = mm_v
                if vsp[st % 2] > 0:
                    c.wait("vector", "sp", vsp[st % 2])
                c.wait("vector", "mm", mm_v)
                ins = None
                if c.on("vector"):
                    ins = nc_.vector.tensor_copy(vstage[st % 2][:], ps[:, 0:256])
                ve_v = c.inc(ins, "ve")
                state["ps_guard"][bank] = ve_v
                c.wait("sync", "ve", ve_v)
                dst = v_s.ap()[2 * dvq:2 * dvq + 2, st].rearrange("g p d -> p g d")
                src3 = vstage[st % 2][:].rearrange("p (g d) -> p g d", g=2)
                vsp[st % 2] = c.dma("sync", "sp", dst, src3)

        if dbg_sdb1 is not None:
            c.dma("sync", "sp", dbg_sdb1.ap(), sdb[:])
        c.barrier()

        vg_guard = [0, 0]
        qh_guard = [0, 0]
        pt_guard = [0, 0, 0, 0]
        stb_ac = [0, 0, 0]
        stb_ve = [0, 0, 0]
        ot_guard = [0, 0]
        den_guard = [0, 0]
        bc_guard = 0
        woq_guard = [0] * 5
        cst_guard = [0, 0, 0, 0]
        last_vgl = 0
        x1sp = 0
        ybuf_ac_guard = 0

        recip_war = [0, 0]
        pending = None

        def emit_norm(pend):
            nonlocal bc_guard
            ph, par, recip_ve = pend
            c.wait("tensor", "ve", recip_ve)
            if bc_guard > 0:
                c.wait("tensor", "ve", bc_guard)
            ins = None
            if c.on("tensor"):
                ins = nc_.tensor.matmul(bc_ps[:], onrb[:], recip[par][:],
                                        start=True, stop=True)
            mm_bc = c.inc(ins, "mm")
            recip_war[par] = mm_bc
            c.wait("vector", "mm", mm_bc)
            if ot_guard[0] > 0 or ot_guard[1] > 0:
                c.wait("vector", "ve", max(ot_guard))
            ins = None
            if c.on("vector"):
                ins = nc_.vector.tensor_copy(recipb[:], bc_ps[:])
            bc_guard = c.inc(ins, "ve")
            c.wait("vector", "ve", bc_guard)
            ins = None
            if c.on("vector"):
                ins = nc_.vector.tensor_mul(
                    OTsl[:, ph * 512:(ph + 1) * 512], ot_ps[par][:], recipb[:])
            nv = c.inc(ins, "ve")
            ot_guard[par] = nv
            return nv

        for qb in range(4):
            last_exp = 0
            last_ve_head = 0
            nkt = 4 * qb + 4

            def load_head(h):
                nonlocal last_vgl
                g = h // 4
                if h % 4 == 0:
                    if vg_guard[g % 2] > 0:
                        c.wait("sync", "mm", vg_guard[g % 2])
                    last_vgl = c.dma("sync", "ld_s",
                                     vg[g % 2][:].rearrange("p (t d) -> p t d", t=16),
                                     v_s.ap()[g].rearrange("t p d -> p t d"))
                if qh_guard[h % 2] > 0:
                    c.wait("sync", "mm", qh_guard[h % 2])
                qhl = c.dma("sync", "ld_s", qh[h % 2][:],
                            qt_s.ap()[h][:, qb * 512:(qb + 1) * 512])
                return qhl, last_vgl

            qh_ld = {0: load_head(0)}
            for h in range(NH):
                g = h // 4

                def emit_score(kt):
                    nonlocal last_exp
                    j = kt - 4 * qb
                    vis0 = 128 * max(j, 0)
                    bank = kt % 3
                    spn = st_ps[bank]
                    if stb_ac[bank] > 0:
                        c.wait("tensor", "ac", stb_ac[bank])
                    if stb_ve[bank] > 0:
                        c.wait("tensor", "ve", stb_ve[bank])
                    ins = None
                    if c.on("tensor"):
                        ins = nc_.tensor.matmul(
                            spn[:, vis0:512],
                            KT[:, g * S + kt * 128: g * S + kt * 128 + 128],
                            qh[h % 2][:, vis0:512], start=True, stop=True)
                    mm_v = c.inc(ins, "mm")
                    ptn = pt[kt % 4]
                    if pt_guard[kt % 4] > 0:
                        c.wait("scalar", "mm", pt_guard[kt % 4])
                    c.wait("scalar", "mm", mm_v)
                    c.wait("scalar", "ve", v_msets)
                    ins = None
                    if c.on("scalar"):
                        ins = nc_.scalar.activation(
                            ptn[:, vis0:512], spn[:, vis0:512], AF.Exp,
                            bias=negB[:], scale=SCALE)
                    ac_v = c.inc(ins, "ac")
                    stb_ac[bank] = ac_v
                    last_exp = ac_v
                    mask_v = 0
                    if j >= 0:
                        c.wait("vector", "ac", ac_v)
                        c.wait("vector", "ld_s", v_tri)
                        ins = None
                        if c.on("vector"):
                            ins = nc_.vector.tensor_mul(
                                ptn[:, vis0:vis0 + 128],
                                ptn[:, vis0:vis0 + 128], trib[:])
                        mask_v = c.inc(ins, "ve")
                    return (ac_v, mask_v, vis0)

                qhl, vgl = qh_ld.pop(h)
                c.wait("tensor", "ld_s", max(qhl, vgl))
                pv_v = 0
                sc = {}
                for kk in range(min(2, nkt)):
                    sc[kk] = emit_score(kk)
                for kt in range(nkt):
                    if kt + 2 < nkt:
                        sc[kt + 2] = emit_score(kt + 2)
                    ac_v, mask_v, vis0 = sc.pop(kt)
                    if mask_v:
                        c.wait("tensor", "ve", mask_v)
                    else:
                        c.wait("tensor", "ac", ac_v)
                    c.wait("tensor", "ld_s", v_onc)
                    if kt == 0:
                        if den_guard[h % 2] > 0:
                            c.wait("tensor", "ve", den_guard[h % 2])
                        if ot_guard[h % 2] > 0:
                            c.wait("tensor", "ve", ot_guard[h % 2])
                    ins = None
                    if c.on("tensor"):
                        nc_.tensor.matmul(
                            den_ps[h % 2][:, vis0:512], oncb[:], pt[kt % 4][:, vis0:512],
                            start=(kt == 0), stop=(kt == nkt - 1))
                        ins = nc_.tensor.matmul(
                            ot_ps[h % 2][:, vis0:512],
                            vg[g % 2][:, kt * 128:kt * 128 + 128],
                            pt[kt % 4][:, vis0:512],
                            start=(kt == 0), stop=(kt == nkt - 1))
                    mm_v = c.inc(ins, "mm")
                    pt_guard[kt % 4] = mm_v
                    pv_v = mm_v
                    if kt == min(1, nkt - 1):
                        if pending is not None:
                            last_ve_head = emit_norm(pending)
                            pending = None
                        if h + 1 < NH:
                            qh_ld[h + 1] = load_head(h + 1)
                qh_guard[h % 2] = pv_v
                if h % 4 == 3:
                    vg_guard[g % 2] = pv_v
                c.wait("vector", "mm", pv_v)
                if recip_war[h % 2] > 0:
                    c.wait("vector", "mm", recip_war[h % 2])
                ins = None
                if c.on("vector"):
                    with nc_.allow_low_precision("1/den rounded to bf16; within the bf16 operand precision class"):
                        ins = nc_.vector.reciprocal(recip[h % 2][:], den_ps[h % 2][:])
                ve_v = c.inc(ins, "ve")
                den_guard[h % 2] = ve_v
                c.wait("vector", "ve", ve_v)
                pending = (h, h % 2, ve_v)
            if pending is not None:
                last_ve_head = emit_norm(pending)
                pending = None

            c.wait("tensor", "ac", last_exp)
            c.wait("tensor", "ve", last_ve_head)
            for cb in range(8):
                qvals = []
                for q in range(4):
                    slot = (cb * 4 + q) % 5
                    ssem = "g_wo%d" % slot
                    if woq_guard[slot] > 0:
                        c.wait("gpsimd", "mm", woq_guard[slot])
                    off = (cb * 4 + q) * 128 * 4096
                    vq = c.dma("gpsimd", ssem, woq[slot][:],
                               wo_t.ap()[off: off + 128 * 4096]
                               .rearrange("(p f) -> p f", p=128))
                    qvals.append((slot, vq))
                for slot, vq in qvals:
                    c.wait("tensor", "g_wo%d" % slot, vq)
                for st in range(4):
                    bank = st % 2
                    ps = st_ps[bank]
                    if stb_ve[bank] > 0:
                        c.wait("tensor", "ve", stb_ve[bank])
                    if cst_guard[st % 4] > 0:
                        c.wait("vector", "sp", cst_guard[st % 4])
                    ins = None
                    if c.on("tensor"):
                        for cc in range(32):
                            slot = (cb * 4 + cc // 8) % 5
                            ins = nc_.tensor.matmul(
                                ps[:],
                                OTsl[:, cc * 512 + st * 128: cc * 512 + st * 128 + 128],
                                woq[slot][:, (cc % 8) * 512:(cc % 8 + 1) * 512],
                                start=(cc == 0), stop=(cc == 31))
                    mm_v = c.inc(ins, "mm")
                    for q in range(4):
                        woq_guard[(cb * 4 + q) % 5] = mm_v
                    c.wait("vector", "mm", mm_v)
                    ins = None
                    if c.on("vector"):
                        ins = nc_.vector.tensor_copy(cstage[st % 4][:], ps[:])
                    ve_v = c.inc(ins, "ve")
                    stb_ve[bank] = ve_v
                    c.wait("sync", "ve", ve_v)
                    rows = qb * 512 + st * 128
                    cst_guard[st % 4] = c.dma(
                        "sync", "sp",
                        y_s.ap()[rows:rows + 128, cb * 512:(cb + 1) * 512],
                        cstage[st % 4][:])
            all_y = cst_guard[3]

            for st in range(4):
                rows = qb * 512 + st * 128
                c.wait("sync", "sp", max(all_y, x1sp))
                if ybuf_ac_guard > 0:
                    c.wait("sync", "ac", ybuf_ac_guard)
                vy = c.dma("sync", "ld_s", ybuf[:], y_s.ap()[rows:rows + 128, :])
                vx = c.dma("sync", "ld_s", xzbuf[:], x_nat.ap()[rows:rows + 128, :])
                c.wait("vector", "ld_s", vx)
                ins = None
                if c.on("vector"):
                    ins = nc_.vector.tensor_add(xzbuf[:], xzbuf[:], ybuf[:])
                ve_za = c.inc(ins, "ve")
                c.wait("vector", "ve", ve_za)
                ins = None
                if c.on("vector"):
                    ins = nc_.vector.reduce_sum(sm["s1"][:], xzbuf[:], AX.X)
                ve_z = c.inc(ins, "ve")
                c.wait("vector", "ve", ve_z)
                c.wait("scalar", "ve", ve_z)
                c.wait("scalar", "sp", x1sp)
                ins = None
                if c.on("scalar"):
                    ins = nc_.scalar.activation(ybuf[:], xzbuf[:], AF.Square,
                                                accum_out=sm["s2"][:])
                ac_sq = c.inc(ins, "ac")
                ybuf_ac_guard = ac_sq
                ins = None
                if c.on("vector"):
                    ins = nc_.vector.tensor_scalar_mul(sm["mean"][:], sm["s1"][:], 1.0 / H)
                ve_mean = c.inc(ins, "ve")
                c.wait("vector", "ve", ve_mean)
                c.wait("scalar", "ve", ve_mean)
                ins = None
                if c.on("scalar"):
                    ins = nc_.scalar.activation(sm["m2"][:], sm["mean"][:], AF.Square)
                ac_m2 = c.inc(ins, "ac")
                c.wait("scalar", "ac", ac_m2)
                c.wait("vector", "ac", max(ac_sq, ac_m2))
                ins = None
                if c.on("vector"):
                    ins = nc_.vector.scalar_tensor_tensor(
                        sm["var"][:], sm["s2"][:], 1.0 / H, sm["m2"][:],
                        ALU.mult, ALU.subtract)
                ve_var = c.inc(ins, "ve")
                c.wait("vector", "ve", ve_var)
                c.wait("scalar", "ve", ve_var)
                ins = None
                if c.on("scalar"):
                    nc_.scalar.activation(sm["lnv"][:], sm["var"][:], AF.Ln, bias=epsb[:])
                    ins = nc_.scalar.activation(sm["rstd"][:], sm["lnv"][:], AF.Exp,
                                                scale=-0.5)
                ac_rstd = c.inc(ins, "ac")
                c.wait("scalar", "ac", ac_rstd)
                c.wait("vector", "ac", ac_rstd)
                ins = None
                if c.on("vector"):
                    ins = nc_.vector.scalar_tensor_tensor(
                        sm["nmr"][:], sm["mean"][:], -1.0, sm["rstd"][:],
                        ALU.mult, ALU.mult)
                ve_nmr = c.inc(ins, "ve")
                c.wait("vector", "ve", ve_nmr)
                c.wait("scalar", "ve", ve_nmr)
                ins = None
                if c.on("scalar"):
                    ins = nc_.scalar.activation(x1buf[:], xzbuf[:], AF.Identity,
                                                scale=sm["rstd"][:], bias=sm["nmr"][:])
                ac_x1 = c.inc(ins, "ac")
                c.wait("sync", "ac", ac_x1)
                x1sp = c.dma("sync", "sp", x1_s.ap()[rows:rows + 128, :], x1buf[:])

        c.barrier()

        x1t_guard = 0
        wg_guard = [0, 0]
        wd_guard = [0, 0]
        h_guard = 0
        gate_guard = 0
        up_guard = 0
        dn_guard = [0, 0]
        out_guard = 0
        out_sp = 0
        x1re_guard = 0
        x1re_ac_guard = 0
        ic_groups = [(0, 11), (11, 22), (22, 33), (33, 44),
                     (44, 55), (55, 66), (66, 77), (77, 86)]
        def load_x1t(ch):
            if x1t_guard > 0:
                c.wait("sync", "mm", x1t_guard)
            v = 0
            for cc in range(32):
                v = c.dma("sync", "ld_s",
                          x1T[:, cc * SB:(cc + 1) * SB],
                          x1_s.ap()[ch * SB:(ch + 1) * SB, cc * 128:(cc + 1) * 128],
                          transpose=True)
            return v

        vxt = load_x1t(0)
        for ch in range(NCH):
            if os.environ.get("KSER_BAR"):
                c.barrier()
            mm_u = 0
            ve_h = 0
            for it in range(NIT):
                slot = it % 2
                ssem = "g_wg%d" % slot
                if wg_guard[slot] > 0:
                    c.wait("gpsimd", "mm", wg_guard[slot])
                c.dma("gpsimd", ssem, wgr[slot][:],
                      wg_t.ap()[it].rearrange("p a b -> p (a b)"))
                vu_w = c.dma("gpsimd", ssem, wur[slot][:],
                             wu_t.ap()[it].rearrange("p a b -> p (a b)"))
                c.wait("tensor", ssem, vu_w)
                c.wait("tensor", "ld_s", vxt)
                if it == 0 and h_guard > 0:
                    c.wait("tensor", "mm", h_guard)
                if gate_guard > 0:
                    c.wait("tensor", "ac", gate_guard)
                ins = None
                if c.on("tensor"):
                    for cc in range(32):
                        ins = nc_.tensor.matmul(
                            gate_ps[:], wgr[slot][:, cc * 128:(cc + 1) * 128],
                            x1T[:, cc * SB:(cc + 1) * SB],
                            start=(cc == 0), stop=(cc == 31))
                mm_g = c.inc(ins, "mm")
                if up_guard > 0:
                    c.wait("tensor", "ve", up_guard)
                ins = None
                if c.on("tensor"):
                    for cc in range(32):
                        ins = nc_.tensor.matmul(
                            up_ps[:], wur[slot][:, cc * 128:(cc + 1) * 128],
                            x1T[:, cc * SB:(cc + 1) * SB],
                            start=(cc == 0), stop=(cc == 31))
                mm_u = c.inc(ins, "mm")
                wg_guard[slot] = mm_u
                c.wait("scalar", "mm", mm_g)
                c.wait("scalar", "ld_s", v_sg)
                if up_guard > 0:
                    c.wait("scalar", "ve", up_guard)
                ins = None
                if c.on("scalar"):
                    ins = nc_.scalar.activation(t1buf[:], gate_ps[:], AF.Silu,
                                                scale=sgb[:, it:it + 1])
                ac_t1 = c.inc(ins, "ac")
                gate_guard = ac_t1
                c.wait("vector", "mm", mm_u)
                c.wait("vector", "ac", ac_t1)
                c.wait("vector", "ld_s", v_su)
                ins = None
                if c.on("vector"):
                    ins = nc_.vector.scalar_tensor_tensor(
                        hstore[:, it * SB:(it + 1) * SB], up_ps[:],
                        sub[:, it:it + 1], t1buf[:], ALU.mult, ALU.mult)
                ve_h = c.inc(ins, "ve")
                up_guard = ve_h
            x1t_guard = mm_u
            if ch + 1 < NCH:
                vxt = load_x1t(ch + 1)
            if os.environ.get("KSER_BAR"):
                c.barrier()
            if dbg_h2 is not None and ch == 0:
                c.wait("sync", "ve", ve_h)
                c.dma("sync", "sp", dbg_h2.ap(), hstore[:])
                c.dma("sync", "sp", dbg_sdb.ap(), sdb[:])
            c.wait("tensor", "ve", ve_h)
            mm_dn = 0
            for ob in range(8):
                for gi, (ic0, ic1) in enumerate(ic_groups):
                    slot = (ob * 8 + gi) % 2
                    ssem = "g_wd%d" % slot
                    if wd_guard[slot] > 0:
                        c.wait("gpsimd", "mm", wd_guard[slot])
                    if os.environ.get("KSER_WD"):
                        c.wait("gpsimd", "mm", c.cnt["mm"])
                    ni = ic1 - ic0
                    off = (ob * NIT + ic0) * 128 * 512
                    src = wd_t.ap()[off: off + 128 * ni * 512].rearrange(
                        "(p f) -> p f", p=128)
                    vd_w = c.dma("gpsimd", ssem,
                                 wdring[slot][:, 0:ni * 512], src)
                    c.wait("tensor", ssem, vd_w)
                    if gi == 0:
                        for st in range(2):
                            if dn_guard[st] > 0:
                                c.wait("tensor", "ve", dn_guard[st])
                    ins = None
                    if c.on("tensor"):
                        for ic in range(ic0, ic1):
                            for st in range(2):
                                ins = nc_.tensor.matmul(
                                    dn_ps[st][:],
                                    hstore[:, ic * SB + st * 128: ic * SB + st * 128 + 128],
                                    wdring[slot][:, (ic - ic0) * 512:(ic - ic0 + 1) * 512],
                                    start=(ic == 0), stop=(ic == NIT - 1))
                    mm_dn = c.inc(ins, "mm")
                    if os.environ.get("KSER_DN"):
                        c.wait("tensor", "mm", mm_dn)
                    wd_guard[slot] = mm_dn
                    if dbg_wd is not None and ch == 0 and ob == 0 and gi < 2:
                        c.wait("sync", "mm", mm_dn)
                        wddump = c.dma("sync", "sp", dbg_wd.ap()[gi], wdring[slot][:])
                        c.wait("gpsimd", "sp", wddump)
                for st in range(2):
                    c.wait("vector", "mm", mm_dn)
                    if dbg_dn is not None and ch == 0 and ob == 0:
                        ins = None
                        if c.on("vector"):
                            ins = nc_.vector.tensor_copy(zbuf[:, st * 512:(st + 1) * 512],
                                                         dn_ps[st][:])
                        dvv = c.inc(ins, "ve")
                        c.wait("sync", "ve", dvv)
                        ddump = c.dma("sync", "sp", dbg_dn.ap()[st],
                                      zbuf[:, st * 512:(st + 1) * 512])
                        c.wait("vector", "sp", ddump)
                    ins = None
                    if c.on("vector"):
                        ins = nc_.vector.tensor_mul(
                            yst[:, st * H + ob * 512: st * H + (ob + 1) * 512],
                            dn_ps[st][:], sdb[:, ob * 512:(ob + 1) * 512])
                    ve_v = c.inc(ins, "ve")
                    dn_guard[st] = ve_v
            y2_done = max(dn_guard)
            h_guard = mm_dn
            if os.environ.get("KSER_BAR"):
                c.barrier()
            if dbg_x1t is not None and ch == 0:
                c.wait("sync", "ve", y2_done)
                c.dma("sync", "sp", dbg_x1t.ap(), x1T[:])
                c.dma("sync", "sp", dbg_h.ap(), hstore[:])
                c.dma("sync", "sp", dbg_y2.ap(), yst[:])

            for st in range(2):
                rows = ch * SB + st * 128
                if x1re_guard > 0:
                    c.wait("sync", "ve", x1re_guard)
                if x1re_ac_guard > 0:
                    c.wait("sync", "ac", x1re_ac_guard)
                vx1 = c.dma("sync", "ld_s", x1re[:], x1_s.ap()[rows:rows + 128, :])
                c.wait("vector", "ld_s", vx1)
                c.wait("vector", "ve", y2_done)
                if out_guard > 0:
                    c.wait("vector", "ac", out_guard)
                ins = None
                if c.on("vector"):
                    ins = nc_.vector.tensor_add(zbuf[:], x1re[:], yst[:, st * H:(st + 1) * H])
                ve_za = c.inc(ins, "ve")
                c.wait("vector", "ve", ve_za)
                ins = None
                if c.on("vector"):
                    ins = nc_.vector.reduce_sum(sm2["s1"][:], zbuf[:], AX.X)
                ve_z = c.inc(ins, "ve")
                x1re_guard = ve_z
                c.wait("vector", "ve", ve_z)
                c.wait("scalar", "ve", ve_z)
                ins = None
                if c.on("scalar"):
                    ins = nc_.scalar.activation(x1re[:], zbuf[:], AF.Square,
                                                accum_out=sm2["s2"][:])
                ac_sq = c.inc(ins, "ac")
                x1re_ac_guard = ac_sq
                ins = None
                if c.on("vector"):
                    ins = nc_.vector.tensor_scalar_mul(sm2["mean"][:], sm2["s1"][:], 1.0 / H)
                ve_mean = c.inc(ins, "ve")
                c.wait("vector", "ve", ve_mean)
                c.wait("scalar", "ve", ve_mean)
                ins = None
                if c.on("scalar"):
                    ins = nc_.scalar.activation(sm2["m2"][:], sm2["mean"][:], AF.Square)
                ac_m2 = c.inc(ins, "ac")
                c.wait("scalar", "ac", ac_m2)
                c.wait("vector", "ac", max(ac_sq, ac_m2))
                ins = None
                if c.on("vector"):
                    ins = nc_.vector.scalar_tensor_tensor(
                        sm2["var"][:], sm2["s2"][:], 1.0 / H, sm2["m2"][:],
                        ALU.mult, ALU.subtract)
                ve_var = c.inc(ins, "ve")
                c.wait("vector", "ve", ve_var)
                c.wait("scalar", "ve", ve_var)
                ins = None
                if c.on("scalar"):
                    nc_.scalar.activation(sm2["lnv"][:], sm2["var"][:], AF.Ln, bias=epsb[:])
                    ins = nc_.scalar.activation(sm2["rstd"][:], sm2["lnv"][:], AF.Exp,
                                                scale=-0.5)
                ac_rstd = c.inc(ins, "ac")
                c.wait("scalar", "ac", ac_rstd)
                c.wait("vector", "ac", ac_rstd)
                ins = None
                if c.on("vector"):
                    ins = nc_.vector.scalar_tensor_tensor(
                        sm2["nmr"][:], sm2["mean"][:], -1.0, sm2["rstd"][:],
                        ALU.mult, ALU.mult)
                ve_nmr = c.inc(ins, "ve")
                c.wait("vector", "ve", ve_nmr)
                c.wait("scalar", "ve", ve_nmr)
                for hf in range(4):
                    if out_guard > 0:
                        c.wait("scalar", "so", out_sp)
                    ins = None
                    if c.on("scalar"):
                        ins = nc_.scalar.activation(
                            outst[:], zbuf[:, hf * 1024:(hf + 1) * 1024], AF.Identity,
                            scale=sm2["rstd"][:], bias=sm2["nmr"][:])
                    ac_o = c.inc(ins, "ac")
                    out_guard = ac_o
                    c.wait("sync", "ac", ac_o)
                    out_sp = c.dma("sync", "so",
                                   out.ap()[rows:rows + 128, hf * 1024:(hf + 1) * 1024],
                                   outst[:])

        c.wait("sync", "so", c.cnt["so"])
        c.barrier()

    return nc, sems, emit


def _round_bf16(a):
    return np.ascontiguousarray(a).astype(BFNP)


def _host_prep(inputs):
    X = np.asarray(inputs["L0_attn_hidden_states"], dtype=np.float32)
    Wq = np.asarray(inputs["L0_attn_Wq"], dtype=np.float32)
    Wk = np.asarray(inputs["L0_attn_Wk"], dtype=np.float32)
    Wv = np.asarray(inputs["L0_attn_Wv"], dtype=np.float32)
    Wo = np.asarray(inputs["L0_attn_Wo"], dtype=np.float32)
    wg = np.asarray(inputs["L0_ffn_w_gate"])
    sg = np.asarray(inputs["L0_ffn_s_gate"], dtype=np.float32)
    wu = np.asarray(inputs["L0_ffn_w_up"])
    su = np.asarray(inputs["L0_ffn_s_up"], dtype=np.float32)
    wd = np.asarray(inputs["L0_ffn_w_down"])
    sd = np.asarray(inputs["L0_ffn_s_down"], dtype=np.float32)

    sh = {}
    sh["wq_t"] = np.ascontiguousarray(Wq.T.reshape(32, 128, 32, 128).transpose(2, 1, 0, 3))
    sh["wk_t"] = np.ascontiguousarray(Wk.T.reshape(32, 128, 8, 128).transpose(2, 1, 0, 3))
    sh["wv_t"] = np.ascontiguousarray(Wv.T.reshape(32, 128, 4, 256).transpose(2, 1, 0, 3))
    WoT_r = Wo.T.reshape(4, 8, 128, 8, 512)
    sh["wo_t"] = np.ascontiguousarray(WoT_r.transpose(3, 0, 2, 1, 4)).reshape(-1)
    sh["wg_t"] = np.ascontiguousarray(wg.T.reshape(32, 128, NIT, 128).transpose(2, 1, 0, 3))
    sh["wu_t"] = np.ascontiguousarray(wu.T.reshape(32, 128, NIT, 128).transpose(2, 1, 0, 3))
    WdT_r = wd.T.reshape(NIT, 128, 8, 512)
    ic_groups = [(0, 11), (11, 22), (22, 33), (33, 44),
                 (44, 55), (55, 66), (66, 77), (77, 86)]
    blocks = []
    for ob in range(8):
        for (ic0, ic1) in ic_groups:
            blocks.append(np.ascontiguousarray(
                WdT_r[ic0:ic1, :, ob, :].transpose(1, 0, 2)).reshape(-1))
    sh["wd_t"] = np.concatenate(blocks)
    sh["sg_t"] = np.ascontiguousarray(sg.reshape(NIT, 128).T)
    sh["su_t"] = np.ascontiguousarray(su.reshape(NIT, 128).T)
    sh["sdb_in"] = np.ascontiguousarray(np.broadcast_to(sd.reshape(1, H), (128, H)))

    inv = 1.0 / 10000.0 ** (np.arange(0, D, 2, dtype=np.float32) / D)
    t = np.arange(S, dtype=np.float32)
    fr = np.outer(t, inv)
    emb = np.concatenate([fr, fr], axis=-1)
    sh["cos_t"] = _round_bf16(np.cos(emb).astype(np.float32).T)
    sin2T = np.sin(emb).astype(np.float32).T.copy()
    sin2T[0:64, :] *= -1.0
    sh["sin_t"] = _round_bf16(sin2T)
    sh["tri_t"] = _round_bf16(np.triu(np.ones((128, 128), dtype=np.float32)))
    sh["onc_t"] = _round_bf16(np.ones((128, 1), dtype=np.float32))
    sh["onr_t"] = np.ones((1, 128), dtype=BFNP)

    in_maps = []
    for b in range(8):
        m = dict(sh)
        m["x_nat"] = np.ascontiguousarray(X[b])
        m["xT"] = np.ascontiguousarray(X[b].T)
        in_maps.append(m)
    return in_maps


def kernel(**inputs):
    in_maps = _host_prep(inputs)
    nc, sems, emit = build_program()
    with nc.Block() as block:
        for name in ENGINES:
            getattr(block, name)(lambda e, n=name: emit(Ctx(nc, n, sems)))
    res = run_bass_kernel_spmd(nc, in_maps, core_ids=list(range(8)))
    outs = [r["out"] for r in res.results]
    return np.stack(outs, axis=0).astype(np.float32)
